# version 01
# baseline (speedup 1.0000x reference)
import numpy as np
import concourse.bass as bass
import concourse.mybir as mybir
from concourse.bass_utils import run_bass_kernel_spmd
from contextlib import ExitStack

F32 = mybir.dt.float32
BF16 = mybir.dt.bfloat16
I8 = mybir.dt.int8
I32 = mybir.dt.int32
AF = mybir.ActivationFunctionType
ALU = mybir.AluOpType
AX = mybir.AxisListType

ENGS = ("sync", "gpsimd", "vector", "scalar", "tensor")

T = 2048
D = 1024
DI = 2048
DS = 16
DTR = 64
DHID = 4096
NH = 16
DH = 64
EPS = 1e-5
WC = 512
NCH = T // WC
WF = 1024
NHF = T // WF

DEV_STAGES = 4
DEBUG = False
DEBUG_CHUNK = 0
MB = 0
DEBUG_BARRIER = False


class Op:
    __slots__ = ("eng", "fn", "deps", "need_inc", "inc_val", "dma", "dma_val",
                 "tag", "xwaits", "need_drain")


class Prog:
    def __init__(self, nc):
        self.nc = nc
        self.ops = []
        self.stream = {e: [] for e in ENGS}
        self.wr = {}
        self.rd = {}
        self.dma_cnt = {}
        self.sems = {}
        self.total_sems = set()

    def q(self, eng):
        return getattr(self.nc, eng)

    def emit(self, eng, fn, reads=(), writes=(), dma=None, deps=(), tag=""):
        op = Op()
        op.eng, op.fn, op.tag = eng, fn, tag
        op.need_inc, op.inc_val, op.dma = False, None, dma
        op.xwaits = ()
        op.need_drain = False
        if dma is not None:
            self.dma_cnt[dma] = self.dma_cnt.get(dma, 0) + 16
            op.dma_val = self.dma_cnt[dma]
        dep = set(deps)
        for s in reads:
            w = self.wr.get(s)
            if w is not None:
                dep.add(w)
        for s in writes:
            w = self.wr.get(s)
            if w is not None:
                dep.add(w)
            for r in self.rd.get(s, ()):
                dep.add(r)
        op.deps = [d for d in dep if d.eng != eng or d.dma is not None]
        if eng in ("vector", "scalar", "gpsimd") and dma is None:
            recent = self.stream[eng][-8:]
            for d in dep:
                if d.eng == eng and d.dma is None and d in recent:
                    op.need_drain = True
                    break
        self.ops.append(op)
        self.stream[eng].append(op)
        for s in reads:
            self.rd.setdefault(s, []).append(op)
        for s in writes:
            self.wr[s] = op
            self.rd[s] = []
        return op

    def barrier(self, with_dma=True):
        lasts = [self.stream[e][-1] for e in ENGS if self.stream[e]]
        nc = self.nc
        fns = {
            "sync": lambda: nc.sync.nop(),
            "tensor": lambda: nc.tensor.nop(),
            "vector": lambda: nc.vector.memset(self.fence_col[:, 0:1], 0.0),
            "scalar": lambda: nc.scalar.activation(
                out=self.fence_col[:, 1:2], in_=self.fence_col[:, 1:2],
                func=mybir.ActivationFunctionType.Identity),
            "gpsimd": lambda: nc.gpsimd.memset(self.fence_col[:, 2:3], 0.0),
        }
        xw = tuple(self.dma_cnt.items()) if with_dma else ()
        for e in ENGS:
            op = self.emit(e, fns[e], deps=lasts, tag="barrier")
            op.xwaits = xw

    def finalize(self, ctx):
        for op in self.ops:
            for d in op.deps:
                if d.dma is None:
                    d.need_inc = True
                elif d.dma in self.total_sems:
                    d.dma_val = self.dma_cnt[d.dma]
        for e in ENGS:
            cnt = 0
            for op in self.stream[e]:
                if op.dma is None and op.need_inc:
                    cnt += 1
                    op.inc_val = cnt
        names = set(self.dma_cnt)
        for e in ENGS:
            if any(op.dma is None and op.need_inc for op in self.stream[e]):
                names.add("eng_" + e)
        for n in sorted(names):
            self.sems[n] = ctx.enter_context(self.nc.semaphore(n))

    def replay_into(self, block):
        for ename in ENGS:
            ops = self.stream[ename]
            if not ops:
                continue

            def runner(_eng, ops=ops, ename=ename):
                hw = {}
                for op in ops:
                    waits = dict(op.xwaits)
                    for d in op.deps:
                        if d.dma is not None:
                            k, v = d.dma, d.dma_val
                        else:
                            k, v = "eng_" + d.eng, d.inc_val
                        if waits.get(k, 0) < v:
                            waits[k] = v
                    for k, v in sorted(waits.items()):
                        if hw.get(k, 0) >= v:
                            continue
                        hw[k] = v
                        self.q(ename).wait_ge(self.sems[k], v)
                    if op.need_drain:
                        self.q(ename).drain()
                    inst = op.fn()
                    if op.dma is not None:
                        inst.then_inc(self.sems[op.dma], 16)
                    elif op.need_inc:
                        inst.then_inc(self.sems["eng_" + ename], 1)
            getattr(block, ename)(runner)


def rearr_cols(ap, p=128):
    return ap.rearrange("(g p) s -> p g s", p=p)


def rearr_vec(ap, p=128):
    return ap.rearrange("(g p) -> p g", p=p)


def colslab(ap, c0, c1, p=128):
    return ap[:, c0:c1].rearrange("(g p) n -> p g n", p=p)


def d3(t, n):
    return t[:, :].rearrange("p (g n) -> p g n", n=n)


def build(nc):
    pr = Prog(nc)
    pr.total_sems.update({"dP", "dPg", "dPv"})
    pr.total_sems.update({"c_S.ip", "c_S.xp", "c_S.dt", "c_S.op", "c_S.w1c",
                          "c_S.w2c", "c_S.h1c", "c_S.h2c", "c_S.w3c", "c_S.wq",
                          "c_S.wk", "c_S.wv"})
    emit = pr.emit

    x_in = nc.dram_tensor("x_in", [T, D], F32, kind="ExternalInput").ap()
    out_t = nc.dram_tensor("out", [T, D], F32, kind="ExternalOutput").ap()
    W = {}
    wshapes = {
        "L0_attn_norm_w": [D], "L0_attn_in_proj_w": [D, 2 * DI],
        "L0_attn_conv_w": [DI, 4], "L0_attn_conv_b": [DI],
        "L0_attn_x_proj_w": [DI, 96], "L0_attn_dt_proj_w": [DTR, DI],
        "L0_attn_dt_proj_b": [DI], "L0_attn_A_log": [DI, DS], "L0_attn_D": [DI],
        "L0_attn_out_proj_w": [DI, D],
        "L0_ffn_w1_w": [DHID, D], "L0_ffn_w1_s": [DHID],
        "L0_ffn_w2_w": [DHID, D], "L0_ffn_w2_s": [DHID],
        "L0_ffn_h1_w": [DHID, DHID], "L0_ffn_h1_s": [DHID],
        "L0_ffn_h2_w": [DHID, DHID], "L0_ffn_h2_s": [DHID],
        "L0_ffn_w3_w": [D, DHID], "L0_ffn_w3_s": [D],
        "L1_attn_ln_g": [D], "L1_attn_ln_b": [D],
        "L1_attn_Wq": [D, D], "L1_attn_bq": [D],
        "L1_attn_Wk": [D, D], "L1_attn_bk": [D],
        "L1_attn_Wv": [D, D], "L1_attn_bv": [D],
        "L1_ffn_w1": [DHID, D], "L1_ffn_conv_w": [DHID, 1, 3],
        "L1_ffn_conv_b": [DHID], "L1_ffn_w2": [D, DHID],
    }
    for n, s in wshapes.items():
        W[n] = nc.dram_tensor(n, s, F32, kind="ExternalInput").ap()

    def dram(name, shape):
        return nc.dram_tensor(name, shape, BF16, kind="Internal").ap()

    ipw_bf = dram("ipw_bf", [D, 2 * DI])
    xpw_bf = dram("xpw_bf", [DI, 96])
    dtw_bf = dram("dtw_bf", [DTR, DI])
    opw_bf = dram("opw_bf", [DI, D])
    wq_bf = dram("wq_bf", [D, D])
    wk_bf = dram("wk_bf", [D, D])
    wv_bf = dram("wv_bf", [D, D])
    w1c = dram("w1c", [DHID, D])
    w2c = dram("w2c", [DHID, D])
    h1c = dram("h1c", [DHID, DHID])
    h2c = dram("h2c", [DHID, DHID])
    w3c = dram("w3c", [D, DHID])
    q1c = dram("q1c", [DHID, D])
    q2c = dram("q2c", [D, DHID])
    w1T = dram("w1T", [D, DHID])
    w2T = dram("w2T", [D, DHID])
    h1T = dram("h1T", [DHID, DHID])
    h2T = dram("h2T", [DHID, DHID])
    w3T = dram("w3T", [DHID, D])
    q1T = dram("q1T", [D, DHID])
    q2T = dram("q2T", [DHID, D])
    xsp = nc.dram_tensor("xsp", [D, T], F32, kind="Internal").ap()

    ctx = ExitStack()
    S = ctx.enter_context

    def sb(name, p, f, dt, st=S):
        return st(nc.sbuf_tensor(name, [p, f], dt))

    psA = [S(nc.psum_tensor(f"PA{i}", [128, 512], F32)) for i in range(4)]
    psB = [S(nc.psum_tensor(f"PB{i}", [128, 1024], F32)) for i in range(2)]
    busy = {"A": [False] * 4, "B": [False] * 2}

    def acq(pool):
        for i, b in enumerate(busy[pool]):
            if not b:
                busy[pool][i] = True
                return i
        raise RuntimeError(f"psum pool {pool} exhausted")

    def rel(pool, i):
        busy[pool][i] = False

    def pa(i):
        return psA[i], f"PA{i}"

    def pb(i):
        return psB[i], f"PB{i}"

    pr.fence_col = sb("fencec", 128, 4, F32)
    idf = sb("idf", 128, 128, F32)
    idi = sb("idi", 128, 128, I32)
    ones_r = sb("ones_r", 1, 128, F32)
    ones_rb = sb("ones_rb", 1, 128, BF16)
    ones_cb = sb("ones_cb", 128, 1, BF16)
    ones_c = sb("ones_c", 128, 1, F32)
    pat = sb("pat", 16, 16 * 128, BF16)
    acols = sb("acols", 128, 16 * DS, F32)
    convw = sb("convw", 128, 16 * 4, F32)
    convb = sb("convb", 128, 16, F32)
    dcol = sb("dcol", 128, 16, F32)
    dtbc = sb("dtbc", 128, 16, F32)
    nrmw = sb("nrmw", 128, 8, F32)
    lngc = sb("lngc", 128, 8, F32)
    lnbc = sb("lnbc", 128, 8, F32)
    bqc = sb("bqc", 128, 8, F32)
    bkc = sb("bkc", 128, 8, F32)
    bvr = sb("bvr", 1, D, BF16)
    s1c = sb("s1c", 128, 32, F32)
    s2c = sb("s2c", 128, 32, F32)
    h1sc = sb("h1sc", 128, 32, F32)
    h2sc = sb("h2sc", 128, 32, F32)
    s3c = sb("s3c", 128, 8, F32)
    cvw1 = sb("cvw1", 128, 32 * 3, F32)
    cvb1 = sb("cvb1", 128, 32, F32)
    qsc = sb("qsc", 128, 4, F32)
    rows = sb("rows", 1, 5 * 1024, F32)
    rowq = sb("rowq", 1, 8, F32)
    t1f = [sb(f"t1f{i}", 128, 512, F32) for i in range(2)]
    LNB = {}

    def row(i, w=512):
        return rows[0:1, i * 1024:i * 1024 + w]

    def dma(eng, out_ap, in_ap, wslots, rslots, sem, tag=""):
        return emit(eng, lambda o=out_ap, i=in_ap, e=eng: pr.q(e).dma_start(out=o, in_=i),
                    reads=rslots, writes=wslots, dma=sem, tag=tag)

    def dmaT(out_ap, in_ap, wslots, rslots, sem="dT", tag=""):
        return emit("sync", lambda o=out_ap, i=in_ap: nc.sync.dma_start_transpose(out=o, in_=i),
                    reads=rslots, writes=wslots, dma=sem, tag=tag)

    cpool = sb("cpool", 128, 8, F32)
    emit("vector", lambda: nc.vector.memset(pr.fence_col[:, :], 0.0),
         writes=["fencec"])
    cvals = [1e-5, 1.0, -0.5, 0.125, -1.0, 0.0, 0.5, 2.0]
    for i, v in enumerate(cvals):
        emit("vector", lambda i=i, v=v: nc.vector.memset(cpool[:, i:i + 1], v),
             writes=["cpool"])
        nc.const_aps.register(F32, v, cpool[:, i:i + 1])
    cfence = pr.wr["cpool"]
    emit("scalar", lambda: nc.scalar.nop(), deps=[cfence], tag="cfence")
    emit("gpsimd", lambda: nc.gpsimd.iota(idi[:, :], pattern=[[1, 128]], base=0,
         channel_multiplier=-1), writes=["idi"])
    emit("vector", lambda: nc.vector.tensor_single_scalar(out=idf[:, :], in_=idi[:, :],
         scalar=0, op=ALU.is_equal), reads=["idi"], writes=["idf"])
    for t_, sl in ((ones_r, "ones_r"), (ones_rb, "ones_rb"), (ones_cb, "ones_cb"),
                   (ones_c, "ones_c")):
        emit("vector", lambda t_=t_: nc.vector.memset(t_[:, :], 1.0), writes=[sl])
    emit("gpsimd", lambda: nc.gpsimd.iota(
         pat[:, :], pattern=[[1, 16], [0, 128]], base=0, channel_multiplier=-1,
         allow_small_or_imprecise_dtypes=True), writes=["pat"])
    emit("vector", lambda: nc.vector.tensor_single_scalar(
         out=pat[:, :], in_=pat[:, :], scalar=0, op=ALU.is_equal),
         reads=["pat"], writes=["pat"])

    vld = sb("vld", 32, 128, F32)

    def load_cols(dst, vec, G, slot):
        dma("sync", vld[0:G, :], vec.rearrange("(g n) -> g n", n=128),
            ["vld"], [], "dV", "vl")
        i = acq("A")
        pt, pts = pa(i)
        emit("tensor", lambda G=G, pt=pt: nc.tensor.transpose(
             out=pt[:, 0:G], in_=vld[0:G, :], identity=idf[0:G, 0:G]),
             reads=["vld", "idf"], writes=[pts], tag="vlT")
        emit("scalar", lambda G=G, pt=pt, dst=dst: nc.scalar.activation(
             out=dst[:, 0:G], in_=pt[:, 0:G], func=AF.Copy),
             reads=[pts], writes=[slot, "vld"], tag="vlE")
        rel("A", i)

    dma("sync", d3(acols, DS), rearr_cols(W["L0_attn_A_log"]), ["acols"], [], "dP")
    dma("sync", d3(convw, 4), rearr_cols(W["L0_attn_conv_w"]), ["convw"], [], "dP")
    load_cols(convb, W["L0_attn_conv_b"], 16, "convb")
    load_cols(dcol, W["L0_attn_D"], 16, "dcol")
    load_cols(dtbc, W["L0_attn_dt_proj_b"], 16, "dtbc")
    load_cols(nrmw, W["L0_attn_norm_w"], 8, "nrmw")
    load_cols(lngc, W["L1_attn_ln_g"], 8, "lngc")
    load_cols(lnbc, W["L1_attn_ln_b"], 8, "lnbc")
    load_cols(bqc, W["L1_attn_bq"], 8, "bqc")
    load_cols(bkc, W["L1_attn_bk"], 8, "bkc")
    load_cols(s1c, W["L0_ffn_w1_s"], 32, "s1c")
    load_cols(s2c, W["L0_ffn_w2_s"], 32, "s2c")
    load_cols(h1sc, W["L0_ffn_h1_s"], 32, "h1sc")
    load_cols(h2sc, W["L0_ffn_h2_s"], 32, "h2sc")
    load_cols(s3c, W["L0_ffn_w3_s"], 8, "s3c")
    dma("sync", d3(cvw1, 3),
        rearr_cols(W["L1_ffn_conv_w"].rearrange("h one k -> (h one) k")),
        ["cvw1"], [], "dP")
    load_cols(cvb1, W["L1_ffn_conv_b"], 32, "cvb1")
    dma("gpsimd", bvr[0:1, :], W["L1_attn_bv"].rearrange("(one d) -> one d", one=1), ["bvr"], [], "dPg")

    emit("scalar", lambda: nc.scalar.activation(out=acols[:, :], in_=acols[:, :],
         func=AF.Exp), reads=["acols"], writes=["acols"])
    emit("vector", lambda: nc.vector.tensor_scalar(out=acols[:, :], in0=acols[:, :],
         scalar1=-1.0, scalar2=None, op0=ALU.mult), reads=["acols"], writes=["acols"])

    stg = ExitStack()
    tb = [sb(f"tb{i}", 128, 1024, BF16, stg.enter_context) for i in range(2)]
    wsl = sb("wsl", 128, 1024, F32, stg.enter_context)
    wsl8 = sb("wsl8", 128, 1024, I8, stg.enter_context)
    wslb = sb("wslb", 128, 1024, BF16, stg.enter_context)
    wacc = sb("wacc", 128, 4, F32, stg.enter_context)

    def stage_cast(dst, src, slot):
        dma("gpsimd", dst[:, :], src, [slot], [], "c_" + slot, "cast")

    def stage_T(dstT, srcC, rn, cn, slot_src, slot_dst):
        for k in range(cn // 128):
            for r0 in range(0, rn, 1024):
                b = (k * ((rn + 1023) // 1024) + r0 // 1024) % 2
                rw = min(1024, rn - r0)
                dmaT(tb[b][:, 0:rw], srcC[r0:r0 + rw, k * 128:(k + 1) * 128],
                     [f"tb{b}"], [slot_src], f"dT{b}", "sT")
                dma("sync", dstT[k * 128:(k + 1) * 128, r0:r0 + rw], tb[b][:, 0:rw],
                    [slot_dst], [f"tb{b}"], f"dT{b}", "sTo")

    stage_cast(ipw_bf, W["L0_attn_in_proj_w"], "S.ip")
    stage_cast(xpw_bf, W["L0_attn_x_proj_w"], "S.xp")
    stage_cast(dtw_bf, W["L0_attn_dt_proj_w"], "S.dt")
    stage_cast(opw_bf, W["L0_attn_out_proj_w"], "S.op")
    stage_cast(w1c, W["L0_ffn_w1_w"], "S.w1c")
    stage_cast(w2c, W["L0_ffn_w2_w"], "S.w2c")
    stage_cast(h1c, W["L0_ffn_h1_w"], "S.h1c")
    stage_cast(h2c, W["L0_ffn_h2_w"], "S.h2c")
    stage_cast(w3c, W["L0_ffn_w3_w"], "S.w3c")
    stage_cast(wq_bf, W["L1_attn_Wq"], "S.wq")
    stage_cast(wk_bf, W["L1_attn_Wk"], "S.wk")
    stage_cast(wv_bf, W["L1_attn_Wv"], "S.wv")
    stage_T(w1T, w1c, DHID, D, "S.w1c", "S.w1T")
    stage_T(w2T, w2c, DHID, D, "S.w2c", "S.w2T")
    stage_T(h1T, h1c, DHID, DHID, "S.h1c", "S.h1T")
    stage_T(h2T, h2c, DHID, DHID, "S.h2c", "S.h2T")
    stage_T(w3T, w3c, D, DHID, "S.w3c", "S.w3T")

    def quant_weight(wsrc, rn, cn, dstC, qi, slot):
        nsl = rn // 128
        ncc = cn // 1024
        emit("vector", lambda: nc.vector.memset(wacc[:, qi:qi + 1], 0.0),
             writes=["wacc"])
        for r in range(nsl):
          for cc in range(ncc):
            dma("sync", wsl[:, :], wsrc[r * 128:(r + 1) * 128, cc * 1024:(cc + 1) * 1024],
                ["wsl"], [], "dL", "qw")
            emit("scalar", lambda: nc.scalar.activation(
                 out=wsl[:, :], in_=wsl[:, :], func=AF.Abs),
                 reads=["wsl"], writes=["wsl"])
            emit("vector", lambda: nc.vector.reduce_sum(
                 out=wacc[:, qi + 2:qi + 3], in_=wsl[:, :], axis=AX.X),
                 reads=["wsl"], writes=["wacc"])
            emit("vector", lambda: nc.vector.tensor_add(
                 out=wacc[:, qi:qi + 1], in0=wacc[:, qi:qi + 1],
                 in1=wacc[:, qi + 2:qi + 3]), reads=["wacc"], writes=["wacc", "wsl"])
        i = acq("A")
        pt, pts = pa(i)
        emit("tensor", lambda pt=pt: nc.tensor.matmul(
             out=pt[0:1, 0:1], lhsT=ones_c[:, :], rhs=wacc[:, qi:qi + 1],
             start=True, stop=True), reads=["wacc", "ones_c"], writes=[pts])
        emit("vector", lambda pt=pt, rn=rn, cn=cn: nc.vector.tensor_scalar(
             out=rowq[0:1, qi:qi + 1], in0=pt[0:1, 0:1],
             scalar1=1.0 / (rn * cn), scalar2=None, op0=ALU.mult),
             reads=[pts], writes=["row15"])
        rel("A", i)
        emit("vector", lambda: nc.vector.reciprocal(
             out=rowq[0:1, 2 + qi:3 + qi], in_=rowq[0:1, qi:qi + 1]),
             reads=["row15"], writes=["row15"])
        i = acq("A")
        pt, pts = pa(i)
        emit("tensor", lambda pt=pt: nc.tensor.matmul(
             out=pt[:, 0:4], lhsT=ones_r[0:1, :],
             rhs=rowq[0:1, 0:4], start=True, stop=True),
             reads=["row15", "ones_r"], writes=[pts])
        emit("vector", lambda pt=pt: nc.vector.tensor_copy(
             out=qsc[:, :], in_=pt[:, 0:4]), reads=[pts], writes=["qsc"])
        rel("A", i)
        for r in range(nsl):
          for cc in range(ncc):
            dma("sync", wsl[:, :], wsrc[r * 128:(r + 1) * 128, cc * 1024:(cc + 1) * 1024],
                ["wsl"], [], "dL", "qw2")
            emit("vector", lambda: nc.vector.tensor_scalar(
                 out=wsl[:, :], in0=wsl[:, :], scalar1=qsc[:, 2 + qi:3 + qi],
                 scalar2=None, op0=ALU.mult), reads=["wsl", "qsc"], writes=["wsl"])
            emit("vector", lambda: nc.vector.tensor_copy(
                 out=wsl8[:, :], in_=wsl[:, :]), reads=["wsl"], writes=["wsl8"])
            emit("vector", lambda: nc.vector.tensor_scalar(
                 out=wslb[:, :], in0=wsl8[:, :], scalar1=-1.0, scalar2=1.0,
                 op0=ALU.max, op1=ALU.min), reads=["wsl8"], writes=["wslb"])
            dma("gpsimd", dstC[r * 128:(r + 1) * 128, cc * 1024:(cc + 1) * 1024],
                wslb[:, :], [slot], ["wslb"], "dQ", "qst")

    emit("vector", lambda: nc.vector.memset(rowq[0:1, :], 1.0), writes=["row15"])
    quant_weight(W["L1_ffn_w1"], DHID, D, q1c, 0, "S.q1c")
    quant_weight(W["L1_ffn_w2"], D, DHID, q2c, 1, "S.q2c")
    stage_T(q1T, q1c, DHID, D, "S.q1c", "S.q1T")
    stage_T(q2T, q2c, D, DHID, "S.q2c", "S.q2T")

    def emit_ln(get_tile, ndc, wlen, apply_cb, extras=None, bf_in=False):
        rb = 0
        pool = "B" if wlen > 512 else "A"
        iS1, iS2 = acq(pool), acq(pool)
        pp = pb if pool == "B" else pa
        p1, p1s = pp(iS1)
        p2, p2s = pp(iS2)
        nk = wlen // 512
        for dc in range(ndc):
            apx, slot = get_tile(dc)
            d = dc % 2
            xht, sqt = LNB["xht"], LNB["sqt"]
            if bf_in:
                hs = slot
                hap = apx
            else:
                hs = f"xht{d}"
                hap = xht[d][:, 0:wlen]
                emit("scalar", lambda apx=apx, d=d, wlen=wlen, xht=xht:
                     nc.scalar.activation(
                     out=xht[d][:, 0:wlen], in_=apx, func=AF.Copy),
                     reads=[slot], writes=[hs], tag="lncp")
            emit("vector", lambda hap=hap, d=d, wlen=wlen, sqt=sqt:
                 nc.vector.tensor_mul(
                 out=sqt[d][:, 0:wlen], in0=hap, in1=hap),
                 reads=[hs], writes=[f"sqt{d}"], tag="lnsq")
            for k in range(nk):
                kk = k * 512 if nk > 1 else 0
                emit("tensor", lambda hap=hap, p1=p1, k=k, s=(dc == 0), e=(dc == ndc - 1):
                     nc.tensor.matmul(out=p1[0:1, k * 512:(k + 1) * 512],
                     lhsT=ones_cb[:, :], rhs=hap[:, k * 512:(k + 1) * 512],
                     start=s, stop=e), reads=[hs, "ones_cb"], writes=[p1s], tag="lnm1")
                emit("tensor", lambda d=d, p2=p2, k=k, s=(dc == 0), e=(dc == ndc - 1),
                     sqt=sqt: nc.tensor.matmul(out=p2[0:1, k * 512:(k + 1) * 512],
                     lhsT=ones_cb[:, :], rhs=sqt[d][:, k * 512:(k + 1) * 512],
                     start=s, stop=e), reads=[f"sqt{d}", "ones_cb"], writes=[p2s],
                     tag="lnm2")
        RM, RV, RR, RE, RQ = rb, rb + 1, rb + 2, rb + 3, rb + 4
        rmu, rv, rr, rmr = (f"row{RM}", f"row{RV}", f"row{RR}", f"row{RE}")
        emit("vector", lambda p1=p1, wlen=wlen: nc.vector.tensor_scalar(
             out=row(RM, wlen), in0=p1[0:1, 0:wlen], scalar1=1.0 / D, scalar2=None,
             op0=ALU.mult), reads=[p1s], writes=[rmu])
        rel(pool, iS1)
        emit("vector", lambda p2=p2, wlen=wlen: nc.vector.tensor_scalar(
             out=row(RV, wlen), in0=p2[0:1, 0:wlen], scalar1=1.0 / D, scalar2=None,
             op0=ALU.mult), reads=[p2s], writes=[rv])
        rel(pool, iS2)
        emit("vector", lambda wlen=wlen: nc.vector.tensor_mul(
             out=row(RR, wlen), in0=row(RM, wlen), in1=row(RM, wlen)),
             reads=[rmu], writes=[rr])
        emit("vector", lambda wlen=wlen: nc.vector.tensor_sub(
             out=row(RV, wlen), in0=row(RV, wlen), in1=row(RR, wlen)),
             reads=[rv, rr], writes=[rv])
        emit("scalar", lambda wlen=wlen: nc.scalar.activation(
             out=row(RR, wlen), in_=row(RV, wlen), func=AF.Ln, bias=EPS),
             reads=[rv], writes=[rr])
        emit("scalar", lambda wlen=wlen: nc.scalar.activation(
             out=row(RR, wlen), in_=row(RR, wlen), func=AF.Exp, scale=-0.5),
             reads=[rr], writes=[rr])
        emit("vector", lambda wlen=wlen: nc.vector.tensor_mul(
             out=row(RE, wlen), in0=row(RM, wlen), in1=row(RR, wlen)),
             reads=[rmu, rr], writes=[rmr])
        if extras == "fold":
            rq = f"row{RQ}"
            emit("vector", lambda wlen=wlen: nc.vector.tensor_mul(
                 out=row(RQ, wlen), in0=row(RV, wlen), in1=row(RR, wlen)),
                 reads=[rv, rr], writes=[rq])
            emit("vector", lambda wlen=wlen: nc.vector.tensor_mul(
                 out=row(RQ, wlen), in0=row(RQ, wlen), in1=row(RR, wlen)),
                 reads=[rq, rr], writes=[rq])
            emit("scalar", lambda wlen=wlen: nc.scalar.activation(
                 out=row(RQ, wlen), in_=row(RQ, wlen), func=AF.Ln, bias=EPS),
                 reads=[rq], writes=[rq])
            emit("scalar", lambda wlen=wlen: nc.scalar.activation(
                 out=row(RQ, wlen), in_=row(RQ, wlen), func=AF.Exp, scale=-0.5),
                 reads=[rq], writes=[rq])
            emit("vector", lambda wlen=wlen: nc.vector.tensor_mul(
                 out=row(RR, wlen), in0=row(RR, wlen), in1=row(RQ, wlen)),
                 reads=[rr, rq], writes=[rr])
            emit("vector", lambda wlen=wlen: nc.vector.tensor_mul(
                 out=row(RE, wlen), in0=row(RE, wlen), in1=row(RQ, wlen)),
                 reads=[rmr, rq], writes=[rmr])
        for k in range(wlen // 512):
            iR, iM = acq("A"), acq("A")
            pR, pRs = pa(iR)
            pM, pMs = pa(iM)
            emit("tensor", lambda pR=pR, k=k: nc.tensor.matmul(
                 out=pR[:, :], lhsT=ones_r[0:1, :],
                 rhs=rows[0:1, RR * 1024 + k * 512:RR * 1024 + (k + 1) * 512],
                 start=True, stop=True), reads=[rr, "ones_r"], writes=[pRs],
                 tag="lnbr")
            emit("tensor", lambda pM=pM, k=k: nc.tensor.matmul(
                 out=pM[:, :], lhsT=ones_r[0:1, :],
                 rhs=rows[0:1, RE * 1024 + k * 512:RE * 1024 + (k + 1) * 512],
                 start=True, stop=True), reads=[rmr, "ones_r"], writes=[pMs],
                 tag="lnbm")
            apply_cb(k, pR, pRs, pM, pMs)
            rel("A", iR)
            rel("A", iM)
        return rb

    pm = ExitStack()

    def sbm(name, p, f, dt):
        return sb(name, p, f, dt, pm.enter_context)

    LNB["xht"] = [sbm(f"xhtM{i}", 128, 512, BF16) for i in range(2)]
    LNB["sqt"] = [sbm(f"sqtM{i}", 128, 512, BF16) for i in range(2)]
    xtok0 = sbm("xtok0", 128, D, F32)
    xtok = [xtok0, xtok0]
    xc = sbm("xc", 128, 8 * WC, F32)
    xnm = sbm("xnm", 128, 8 * WC, BF16)
    ipwb0 = sbm("ipwb0", 128, 8 * 128, BF16)
    ipwb = [ipwb0, ipwb0]
    upad = sbm("upad", 128, 16 * (WC + 3), BF16)
    srmo = sbm("srmo", 128, 16 * WC, BF16)
    ucv = sbm("ucv", 128, 16 * WC, BF16)
    delta = sbm("delta", 128, 16 * WC, BF16)
    duv = sbm("duv", 128, 16 * WC, BF16)
    yac2 = sbm("yac2", 128, 8 * WC, BF16)
    yac1 = sbm("yac1", 128, 8 * WC, BF16)
    dtq = sbm("dtq", 64, WC, BF16)
    brow = sbm("brow", 16, WC, BF16)
    crow = sbm("crow", 16, WC, BF16)
    xpwsb = sbm("xpwsb", 128, 16 * 96, BF16)
    dtwsb = sbm("dtwsb", 64, DI, BF16)
    opwb0 = sbm("opwb0", 128, 16 * 128, BF16)
    opwb = [opwb0, opwb0]
    hcar = sbm("hcar", 128, 256, F32)
    dma("sync", d3(xpwsb, 96), rearr_cols(xpw_bf), ["xpwsb"], ["S.xp"], "d_xpw")
    dma("sync", dtwsb[:, :], dtw_bf[:, :], ["dtwsb"], ["S.dt"], "d_dtw")
    bexs = sbm("bexs", 128, WC, BF16)
    cexs = sbm("cexs", 128, WC, BF16)
    abuf = [sbm(f"abuf{i}", 128, WC, F32) for i in range(2)]
    bbuf = [sbm(f"bbuf{i}", 128, WC, BF16) for i in range(2)]
    hbuf = [sbm(f"hbuf{i}", 128, WC, BF16) for i in range(2)]
    tcb = [sbm(f"tcb{i}", 128, WC, BF16) for i in range(2)]
    facc = [sbm("facc0", 128, WC, F32)]
    facc.append(facc[0])

    emit("vector", lambda: nc.vector.memset(hcar[:, :], 0.0), writes=["hcar"])
    for g in range(16):
        emit("vector", lambda g=g: nc.vector.memset(
             upad[:, g * (WC + 3):g * (WC + 3) + 3], 0.0), writes=[f"up{g}"])

    def upg(g):
        return upad[:, g * (WC + 3):(g + 1) * (WC + 3)]

    def fm(t, i, w=WC):
        return t[:, i * w:(i + 1) * w]

    def yact(g):
        return fm(yac1, g) if g < 8 else fm(yac2, g - 8)

    def yslot(g):
        return f"yac1{g}" if g < 8 else f"yac2{g - 8}"

    dbg = {}
    if DEBUG:
        for nm, shp, dt_ in (("d_xc", [128, 8 * WC], F32), ("d_xnm", [128, 8 * WC], BF16),
                             ("d_u", [128, 16 * (WC + 3)], BF16),
                             ("d_sr", [128, 16 * WC], BF16), ("d_uc", [128, 16 * WC], BF16),
                             ("d_dtq", [64, WC], BF16), ("d_brow", [16, WC], BF16),
                             ("d_crow", [16, WC], BF16), ("d_delta", [128, 16 * WC], BF16),
                             ("d_y", [128, 16 * WC], BF16), ("d_yf", [128, 16 * WC], BF16),
                             ("d_mo", [128, 8 * WC], BF16),
                             ("d_opw", [128, 16 * 128], BF16),
                             ("d_x1", [128, 8 * WC], F32)):
            dbg[nm] = nc.dram_tensor(nm, shp, dt_, kind="ExternalOutput").ap()

    def dump(nm, ap, rslots):
        if DEBUG:
            dma("gpsimd", dbg[nm][:, :], ap, [f"dbg{nm}"], rslots, "dO_" + nm, "dbg")

    for c in range(NCH):
        if DEBUG_BARRIER:
            pr.barrier(with_dma=(DEBUG_BARRIER == 1))
        for j in range(4):
            jj = j % 2
            dma("sync", xtok[jj][:, :], x_in[(c * 4 + j) * 128:(c * 4 + j + 1) * 128, :],
                ["xtok0"], [], "x_t0")
            for dc in range(8):
                i = acq("A")
                pt, pts = pa(i)
                emit("tensor", lambda jj=jj, dc=dc, pt=pt: nc.tensor.transpose(
                     out=pt[:, 0:128], in_=xtok[jj][:, dc * 128:(dc + 1) * 128],
                     identity=idf[:, :]), reads=["xtok0", "idf"], writes=[pts],
                     tag="xT")
                emit("scalar", lambda dc=dc, j=j, pt=pt: nc.scalar.activation(
                     out=fm(xc, dc)[:, j * 128:(j + 1) * 128], in_=pt[:, 0:128],
                     func=AF.Copy), reads=[pts], writes=[f"xc{dc}"], tag="xTe")
                rel("A", i)

        def apply1(k, pR, pRs, pM, pMs):
            for dc in range(8):
                d = dc % 2
                emit("vector", lambda dc=dc, k=k, pR=pR, d=d: nc.vector.tensor_mul(
                     out=t1f[d][:, :], in0=fm(xc, dc)[:, k * 512:(k + 1) * 512],
                     in1=pR[:, :]), reads=[f"xc{dc}", pRs], writes=[f"t1f{d}"])
                emit("vector", lambda dc=dc, k=k, pM=pM, d=d: nc.vector.tensor_sub(
                     out=t1f[d][:, :], in0=t1f[d][:, :], in1=pM[:, :]),
                     reads=[f"t1f{d}", pMs], writes=[f"t1f{d}"])
                emit("vector", lambda dc=dc, k=k, d=d: nc.vector.tensor_scalar(
                     out=fm(xnm, dc)[:, k * 512:(k + 1) * 512], in0=t1f[d][:, :],
                     scalar1=nrmw[:, dc:dc + 1], scalar2=None, op0=ALU.mult),
                     reads=[f"t1f{d}", "nrmw"], writes=[f"xnm{dc}"])
        emit_ln(lambda dc: (fm(xc, dc), f"xc{dc}"), 8, WC, apply1, extras="fold")
        if c == DEBUG_CHUNK:
            dump("d_xc", xc[:, :], [f"xc{i}" for i in range(8)])
            dump("d_xnm", xnm[:, :], [f"xnm{i}" for i in range(8)])
        if MB & 1:
            pr.barrier()
        for oc in range(32):
            b = oc % 2
            dma("sync", d3(ipwb[b], 128), colslab(ipw_bf, oc * 128, (oc + 1) * 128),
                ["ipwb0"], ["S.ip"], "w_ip0", "ipld")
            i = acq("A")
            pt, pts = pa(i)
            for ks in range(8):
                emit("tensor", lambda b=b, ks=ks, pt=pt, s=(ks == 0),
                     e=(ks == 7): nc.tensor.matmul(out=pt[:, :],
                     lhsT=ipwb[b][:, ks * 128:(ks + 1) * 128],
                     rhs=fm(xnm, ks), start=s, stop=e),
                     reads=["ipwb0", f"xnm{ks}"], writes=[pts], tag="ipmm")
            if oc < 16:
                emit("scalar", lambda oc=oc, pt=pt: nc.scalar.activation(
                     out=upg(oc)[:, 3:3 + WC], in_=pt[:, :], func=AF.Copy),
                     reads=[pts], writes=[f"up{oc}"], tag="uev")
            else:
                emit("scalar", lambda oc=oc, pt=pt: nc.scalar.activation(
                     out=fm(srmo, oc - 16), in_=pt[:, :], func=AF.Silu),
                     reads=[pts], writes=[f"srmo{oc - 16}"], tag="srev")
            rel("A", i)
        if MB & 2:
            pr.barrier()
        for g in range(16):
            bb = g % 2
            emit("vector", lambda g=g, bb=bb: nc.vector.tensor_scalar(
                 out=facc[bb][:, :], in0=upg(g)[:, 0:WC],
                 scalar1=convw[:, g * 4:g * 4 + 1], scalar2=None, op0=ALU.mult),
                 reads=[f"up{g}", "convw"], writes=["facc"])
            for k in range(1, 4):
                emit("vector", lambda g=g, k=k, bb=bb: nc.vector.scalar_tensor_tensor(
                     out=facc[bb][:, :], in0=upg(g)[:, k:k + WC],
                     scalar=convw[:, g * 4 + k:g * 4 + k + 1], in1=facc[bb][:, :],
                     op0=ALU.mult, op1=ALU.add),
                     reads=[f"up{g}", "convw", "facc"], writes=["facc"])
            emit("scalar", lambda g=g, bb=bb: nc.scalar.activation(
                 out=fm(ucv, g), in_=facc[bb][:, :], func=AF.Silu,
                 bias=convb[:, g:g + 1]),
                 reads=["facc", "convb"], writes=[f"ucv{g}"], tag="cvsl")
            if c < NCH - 1:
                emit("vector", lambda g=g: nc.vector.tensor_copy(
                     out=upg(g)[:, 0:3], in_=upg(g)[:, WC:WC + 3]),
                     reads=[f"up{g}"], writes=[f"up{g}"], tag="halo")
        if c == DEBUG_CHUNK:
            dump("d_u", upad[:, :], [f"up{i}" for i in range(16)])
            dump("d_sr", srmo[:, :], [f"srmo{i}" for i in range(16)])
            dump("d_uc", ucv[:, :], [f"ucv{i}" for i in range(16)])
        if MB & 4:
            pr.barrier()
        idt, ib, ic = acq("A"), acq("A"), acq("A")
        pdt, pdts = pa(idt)
        pbm, pbms = pa(ib)
        pcm, pcms = pa(ic)
        for ks in range(16):
            emit("tensor", lambda ks=ks, p=pdt, s=(ks == 0), e=(ks == 15):
                 nc.tensor.matmul(out=p[0:64, :], lhsT=xpwsb[:, ks * 96:ks * 96 + 64],
                 rhs=fm(ucv, ks), start=s, stop=e),
                 reads=["xpwsb", f"ucv{ks}"], writes=[pdts], tag="xpdt")
            emit("tensor", lambda ks=ks, p=pbm, s=(ks == 0), e=(ks == 15):
                 nc.tensor.matmul(out=p[0:16, :], lhsT=xpwsb[:, ks * 96 + 64:ks * 96 + 80],
                 rhs=fm(ucv, ks), start=s, stop=e),
                 reads=["xpwsb", f"ucv{ks}"], writes=[pbms], tag="xpB")
            emit("tensor", lambda ks=ks, p=pcm, s=(ks == 0), e=(ks == 15):
                 nc.tensor.matmul(out=p[0:16, :], lhsT=xpwsb[:, ks * 96 + 80:ks * 96 + 96],
                 rhs=fm(ucv, ks), start=s, stop=e),
                 reads=["xpwsb", f"ucv{ks}"], writes=[pcms], tag="xpC")
        emit("scalar", lambda p=pdt: nc.scalar.activation(
             out=dtq[:, :], in_=p[0:64, :], func=AF.Copy), reads=[pdts],
             writes=["dtq"])
        emit("scalar", lambda p=pbm: nc.scalar.activation(
             out=brow[:, :], in_=p[0:16, :], func=AF.Copy), reads=[pbms],
             writes=["brow"])
        emit("scalar", lambda p=pcm: nc.scalar.activation(
             out=crow[:, :], in_=p[0:16, :], func=AF.Copy), reads=[pcms],
             writes=["crow"])
        rel("A", idt)
        rel("A", ib)
        rel("A", ic)
        for g in range(16):
            bb = g % 2
            i = acq("A")
            pt, pts = pa(i)
            emit("tensor", lambda g=g, pt=pt: nc.tensor.matmul(
                 out=pt[:, :], lhsT=dtwsb[:, g * 128:(g + 1) * 128], rhs=dtq[:, :],
                 start=True, stop=True), reads=["dtwsb", "dtq"], writes=[pts],
                 tag="dtmm")
            emit("scalar", lambda g=g, pt=pt, bb=bb: nc.scalar.activation(
                 out=facc[bb][:, :], in_=pt[:, :], func=AF.Exp,
                 bias=dtbc[:, g:g + 1]), reads=[pts, "dtbc"], writes=["facc"],
                 tag="dexp")
            rel("A", i)
            emit("scalar", lambda g=g, bb=bb: nc.scalar.activation(
                 out=fm(delta, g), in_=facc[bb][:, :], func=AF.Ln, bias=1.0),
                 reads=["facc"], writes=[f"dl{g}"], tag="dln")
            emit("vector", lambda g=g: nc.vector.tensor_mul(
                 out=fm(duv, g), in0=fm(delta, g), in1=fm(ucv, g)),
                 reads=[f"dl{g}", f"ucv{g}"], writes=[f"du{g}"])
        if c == DEBUG_CHUNK:
            dump("d_dtq", dtq[:, :], ["dtq"])
            dump("d_brow", brow[:, :], ["brow"])
            dump("d_crow", crow[:, :], ["crow"])
            dump("d_delta", delta[:, :], [f"dl{i}" for i in range(16)])
        if MB & 8:
            pr.barrier()
        for s_ in range(16):
            sbb = s_ % 2
            iB, iC = acq("A"), acq("A")
            pB, pBs = pa(iB)
            pC, pCs = pa(iC)
            emit("tensor", lambda s_=s_, p=pB: nc.tensor.matmul(
                 out=p[:, :], lhsT=pat[:, s_ * 128:(s_ + 1) * 128], rhs=brow[:, :],
                 start=True, stop=True), reads=["brow", "pat"], writes=[pBs],
                 tag="bexm")
            emit("tensor", lambda s_=s_, p=pC: nc.tensor.matmul(
                 out=p[:, :], lhsT=pat[:, s_ * 128:(s_ + 1) * 128], rhs=crow[:, :],
                 start=True, stop=True), reads=["crow", "pat"], writes=[pCs],
                 tag="cexm")
            emit("scalar", lambda p=pB: nc.scalar.activation(
                 out=bexs[:, :], in_=p[:, :], func=AF.Copy), reads=[pBs],
                 writes=["bex"], tag="bexe")
            emit("scalar", lambda p=pC: nc.scalar.activation(
                 out=cexs[:, :], in_=p[:, :], func=AF.Copy), reads=[pCs],
                 writes=["cex"], tag="cexe")
            rel("A", iB)
            rel("A", iC)
            for g in range(16):
                gb = g % 2
                col = g * 16 + s_
                emit("scalar", lambda g=g, s_=s_, gb=gb: nc.scalar.activation(
                     out=abuf[gb][:, :], in_=fm(delta, g), func=AF.Exp,
                     scale=acols[:, g * 16 + s_:g * 16 + s_ + 1]),
                     reads=[f"dl{g}", "acols"], writes=[f"ab{gb}"], tag="sexp")
                emit("vector", lambda g=g, gb=gb: nc.vector.tensor_mul(
                     out=bbuf[gb][:, :], in0=fm(duv, g), in1=bexs[:, :]),
                     reads=[f"du{g}", "bex"], writes=[f"bb{gb}"])
                emit("vector", lambda gb=gb, col=col: nc.vector.tensor_tensor_scan(
                     out=hbuf[gb][:, :], data0=abuf[gb][:, :], data1=bbuf[gb][:, :],
                     initial=hcar[:, col:col + 1], op0=ALU.mult, op1=ALU.add),
                     reads=[f"ab{gb}", f"bb{gb}", "hcar"], writes=[f"hb{gb}"],
                     tag="scan")
                if c < NCH - 1:
                    emit("vector", lambda gb=gb, col=col: nc.vector.tensor_copy(
                         out=hcar[:, col:col + 1], in_=hbuf[gb][:, WC - 1:WC]),
                         reads=[f"hb{gb}"], writes=["hcar"], tag="carry")
                emit("vector", lambda gb=gb: nc.vector.tensor_mul(
                     out=tcb[gb][:, :], in0=hbuf[gb][:, :], in1=cexs[:, :]),
                     reads=[f"hb{gb}", "cex"], writes=[f"tc{gb}"])
                if s_ == 0:
                    emit("vector", lambda g=g, gb=gb: nc.vector.tensor_copy(
                         out=yact(g), in_=tcb[gb][:, :]),
                         reads=[f"tc{gb}"], writes=[yslot(g)], tag="yin")
                else:
                    emit("vector", lambda g=g, gb=gb: nc.vector.tensor_add(
                         out=yact(g), in0=yact(g), in1=tcb[gb][:, :]),
                         reads=[f"tc{gb}", yslot(g)], writes=[yslot(g)], tag="yadd")
                if DEBUG and c == DEBUG_CHUNK and g < 2:
                    if "d_ys" not in dbg:
                        dbg["d_ys"] = nc.dram_tensor(
                            "d_ys", [128, 16 * 2 * WC], BF16,
                            kind="ExternalOutput").ap()
                    dma("gpsimd",
                        dbg["d_ys"][:, (s_ * 2 + g) * WC:(s_ * 2 + g + 1) * WC],
                        yact(g), [f"dbgys{s_}_{g}"], [yslot(g)],
                        f"dOs{g}", "dbg")
        if c == DEBUG_CHUNK and DEBUG:
            dma("gpsimd", dbg["d_y"][:, 0:8 * WC], yac1[:, :], ["dbgd_y"],
                [yslot(i) for i in range(8)], "dO1", "dbg")
            dma("gpsimd", dbg["d_y"][:, 8 * WC:16 * WC], yac2[:, :], ["dbgd_y"],
                [yslot(8 + i) for i in range(8)], "dO2", "dbg")
        if c == DEBUG_CHUNK and DEBUG:
            pass
        if MB & 16:
            pr.barrier()
        for g in range(16):
            emit("vector", lambda g=g: nc.vector.scalar_tensor_tensor(
                 out=yact(g), in0=fm(ucv, g), scalar=dcol[:, g:g + 1],
                 in1=yact(g), op0=ALU.mult, op1=ALU.add),
                 reads=[yslot(g), f"ucv{g}", "dcol"], writes=[yslot(g)])
            emit("vector", lambda g=g: nc.vector.tensor_mul(
                 out=yact(g), in0=yact(g), in1=fm(srmo, g)),
                 reads=[yslot(g), f"srmo{g}"], writes=[yslot(g)])
        if c == DEBUG_CHUNK and DEBUG:
            dma("gpsimd", dbg["d_yf"][:, 0:8 * WC], yac1[:, :], ["dbgd_yf"],
                [yslot(i) for i in range(8)], "dO3", "dbg")
            dma("gpsimd", dbg["d_yf"][:, 8 * WC:16 * WC], yac2[:, :], ["dbgd_yf"],
                [yslot(8 + i) for i in range(8)], "dO4", "dbg")
        if MB & 32:
            pr.barrier()
        emit("vector", lambda: nc.vector.memset(pr.fence_col[:, 3:4], 0.0),
             reads=[yslot(g) for g in range(16)], writes=["yfence"], tag="yfence")
        for oc in range(8):
            b = oc % 2
            dma("sync", d3(opwb[b], 128), colslab(opw_bf, oc * 128, (oc + 1) * 128),
                ["opwb0"], ["S.op"], "w_op0", "opld")
            if c == DEBUG_CHUNK and oc == 0 and DEBUG:
                dma("gpsimd", dbg["d_opw"][:, :], opwb[0][:, :], ["dbgow"],
                    ["opwb0"], "dO5", "dbg")
            i = acq("A")
            pt, pts = pa(i)
            for ks in range(16):
                emit("tensor", lambda b=b, ks=ks, pt=pt, s=(ks == 0), e=(ks == 15):
                     nc.tensor.matmul(out=pt[:, :],
                     lhsT=opwb[b][:, ks * 128:(ks + 1) * 128], rhs=yact(ks),
                     start=s, stop=e),
                     reads=["opwb0", yslot(ks), "yfence"], writes=[pts],
                     tag="opmm")
            emit("scalar", lambda oc=oc, pt=pt: nc.scalar.activation(
                 out=fm(srmo, oc), in_=pt[:, :], func=AF.Copy),
                 reads=[pts], writes=[f"srmo{oc}"], tag="moev")
            rel("A", i)

        def apply2(k, pR, pRs, pM, pMs):
            for dc in range(8):
                d = dc % 2
                emit("vector", lambda dc=dc, k=k, pR=pR, d=d: nc.vector.tensor_mul(
                     out=t1f[d][:, :], in0=fm(srmo, dc)[:, k * 512:(k + 1) * 512],
                     in1=pR[:, :]), reads=[f"srmo{dc}", pRs], writes=[f"t1f{d}"])
                emit("vector", lambda k=k, pM=pM, d=d: nc.vector.tensor_sub(
                     out=t1f[d][:, :], in0=t1f[d][:, :], in1=pM[:, :]),
                     reads=[f"t1f{d}", pMs], writes=[f"t1f{d}"])
                emit("vector", lambda dc=dc, k=k, d=d: nc.vector.tensor_add(
                     out=fm(xc, dc)[:, k * 512:(k + 1) * 512],
                     in0=fm(xc, dc)[:, k * 512:(k + 1) * 512], in1=t1f[d][:, :]),
                     reads=[f"t1f{d}", f"xc{dc}"], writes=[f"xc{dc}"])
        if c == DEBUG_CHUNK:
            dump("d_mo", srmo[:, 0:8 * WC], [f"srmo{i}" for i in range(8)])
        if MB & 64:
            pr.barrier()
        emit_ln(lambda dc: (fm(srmo, dc), f"srmo{dc}"), 8, WC, apply2, bf_in=True)
        if c == DEBUG_CHUNK:
            dump("d_x1", xc[:, :], [f"xc{i}" for i in range(8)])
        dma("gpsimd", xsp.rearrange("(g p) t -> p g t", p=128)[:, :, c * WC:(c + 1) * WC],
            xc[:, :].rearrange("p (g t) -> p g t", g=8),
            [f"sp{c}"], [f"xc{dc}" for dc in range(8)], "s_xc", "spill")

    pr.barrier()
    pm.close()
    stg.close()

    if DEV_STAGES >= 2:
        pf = ExitStack()

        def sbff(name, p, f, dt):
            return sb(name, p, f, dt, pf.enter_context)

        LNB["xht"] = [sbff(f"xhtF{i}", 128, 1024, BF16) for i in range(2)]
        LNB["sqt"] = [sbff(f"sqtF{i}", 128, 1024, BF16) for i in range(2)]
        xrot = [sbff(f"xrot{i}", 128, WF, F32) for i in range(2)]
        hful = sbff("hful", 128, 32 * WF, BF16)
        gful = sbff("gful", 128, 32 * WF, BF16)
        wsb = [sbff(f"wsb{i}", 128, 8 * 256, BF16) for i in range(2)]
        hwb = [sbff(f"hwb{i}", 128, 32 * 128, BF16) for i in range(2)]
        tmp1 = sbff("tmp1", 128, 2 * WF, BF16)

        def xnf(i, w=WF):
            return gful[:, i * w:(i + 1) * w]

        for hh in range(NHF):
            def get_xf(dc, hh=hh):
                d = dc % 2
                dma("sync", xrot[d][:, :],
                    xsp[dc * 128:(dc + 1) * 128, hh * WF:(hh + 1) * WF],
                    [f"xrot{d}"], [], f"x_r{d}", "xfld")
                return (xrot[d][:, :], f"xrot{d}")
            emit_ln(get_xf, 8, WF, lambda *a: None)
            ib4 = [acq("A") for _ in range(4)]
            pb4 = [pa(i) for i in ib4]
            for k in range(2):
                pR, pRs = pb4[2 * k]
                pM, pMs = pb4[2 * k + 1]
                emit("tensor", lambda pR=pR, k=k: nc.tensor.matmul(
                     out=pR[:, :], lhsT=ones_r[0:1, :],
                     rhs=rows[0:1, 2 * 1024 + k * 512:2 * 1024 + (k + 1) * 512],
                     start=True, stop=True), reads=["row2", "ones_r"], writes=[pRs])
                emit("tensor", lambda pM=pM, k=k: nc.tensor.matmul(
                     out=pM[:, :], lhsT=ones_r[0:1, :],
                     rhs=rows[0:1, 3 * 1024 + k * 512:3 * 1024 + (k + 1) * 512],
                     start=True, stop=True), reads=["row3", "ones_r"], writes=[pMs])
            for dc in range(8):
                d = dc % 2
                dma("sync", xrot[d][:, :],
                    xsp[dc * 128:(dc + 1) * 128, hh * WF:(hh + 1) * WF],
                    [f"xrot{d}"], [], f"x_r{d}", "xfld2")
                for k in range(2):
                    pR, pRs = pb4[2 * k]
                    pM, pMs = pb4[2 * k + 1]
                    emit("vector", lambda dc=dc, k=k, pR=pR, d=d: nc.vector.tensor_mul(
                         out=t1f[d][:, :], in0=xrot[d][:, k * 512:(k + 1) * 512],
                         in1=pR[:, :]), reads=[f"xrot{d}", pRs], writes=[f"t1f{d}"])
                    emit("vector", lambda dc=dc, k=k, pM=pM, d=d: nc.vector.tensor_sub(
                         out=xnf(dc)[:, k * 512:(k + 1) * 512], in0=t1f[d][:, :],
                         in1=pM[:, :]), reads=[f"t1f{d}", pMs], writes=[f"gf{dc}"])
            for i in ib4:
                rel("A", i)
            if hh == 0 and DEBUG:
                for nm in ("f_xnf", "f_h", "f_g1", "f_g2", "f_fo"):
                    if nm not in dbg:
                        dbg[nm] = nc.dram_tensor(nm, [128, 32 * WF], BF16,
                                                 kind="ExternalOutput").ap()
                dma("gpsimd", dbg["f_xnf"][:, 0:8 * WF], gful[:, 0:8 * WF],
                    ["dbgf1"], [f"gf{i}" for i in range(8)], "dO6", "dbg")
            for ocb in range(16):
                b = ocb % 2
                dma("sync", d3(wsb[b], 256), colslab(w1T, ocb * 256, (ocb + 1) * 256),
                    [f"wsb{b}"], ["S.w1T"], f"w_f{b}", "w1ld")
                for oci in range(2):
                    oc = ocb * 2 + oci
                    iB = acq("B")
                    ptB, ptBs = pb(iB)
                    for ks in range(8):
                        for k in range(2):
                            emit("tensor", lambda b=b, oci=oci, ks=ks, k=k, pt=ptB,
                                 s=(ks == 0), e=(ks == 7): nc.tensor.matmul(
                                 out=pt[:, k * 512:(k + 1) * 512],
                                 lhsT=wsb[b][:, ks * 256 + oci * 128:ks * 256 + (oci + 1) * 128],
                                 rhs=xnf(ks)[:, k * 512:(k + 1) * 512],
                                 start=s, stop=e),
                                 reads=[f"wsb{b}", f"gf{ks}"], writes=[ptBs],
                                 tag="w1mm")
                    emit("scalar", lambda oc=oc, oci=oci, pt=ptB: nc.scalar.activation(
                         out=tmp1[:, oci * WF:(oci + 1) * WF], in_=pt[:, :],
                         func=AF.Silu, scale=s1c[:, oc:oc + 1]),
                         reads=[ptBs, "s1c"], writes=[f"tmp1{oci}"], tag="w1ev")
                    rel("B", iB)
                b2 = (ocb + 1) % 2
                dma("sync", d3(wsb[b2], 256), colslab(w2T, ocb * 256, (ocb + 1) * 256),
                    [f"wsb{b2}"], ["S.w2T"], f"w_f{b2}", "w2ld")
                for oci in range(2):
                    oc = ocb * 2 + oci
                    iB = acq("B")
                    ptB, ptBs = pb(iB)
                    for ks in range(8):
                        for k in range(2):
                            emit("tensor", lambda b2=b2, oci=oci, ks=ks, k=k, pt=ptB,
                                 s=(ks == 0), e=(ks == 7): nc.tensor.matmul(
                                 out=pt[:, k * 512:(k + 1) * 512],
                                 lhsT=wsb[b2][:, ks * 256 + oci * 128:ks * 256 + (oci + 1) * 128],
                                 rhs=xnf(ks)[:, k * 512:(k + 1) * 512],
                                 start=s, stop=e),
                                 reads=[f"wsb{b2}", f"gf{ks}"], writes=[ptBs],
                                 tag="w2mm")
                    for k in range(2):
                        emit("vector", lambda oc=oc, oci=oci, k=k, pt=ptB:
                             nc.vector.scalar_tensor_tensor(
                             out=fm(hful, oc, WF)[:, k * 512:(k + 1) * 512],
                             in0=pt[:, k * 512:(k + 1) * 512],
                             scalar=s2c[:, oc:oc + 1],
                             in1=tmp1[:, oci * WF + k * 512:oci * WF + (k + 1) * 512],
                             op0=ALU.mult, op1=ALU.mult),
                             reads=[ptBs, "s2c", f"tmp1{oci}"], writes=[f"hf{oc}"],
                             tag="hmul")
                    rel("B", iB)
            if hh == 0 and DEBUG:
                dma("gpsimd", dbg["f_h"][:, :], hful[:, :], ["dbgf2"],
                    [f"hf{i}" for i in range(32)], "dO7", "dbg")
            for (wt, wslot, src, srcs, dst, dsts, scol, scs) in (
                    (h1T, "S.h1T", hful, "hf", gful, "gf", h1sc, "h1sc"),
                    (h2T, "S.h2T", gful, "gf", hful, "hf", h2sc, "h2sc")):
                for oc2 in range(32):
                    b = oc2 % 2
                    dma("sync", d3(hwb[b], 128), colslab(wt, oc2 * 128, (oc2 + 1) * 128),
                        [f"hwb{b}"], [wslot], f"w_h{b}", "hld")
                    iB = acq("B")
                    ptB, ptBs = pb(iB)
                    for ks in range(32):
                        for k in range(2):
                            emit("tensor", lambda b=b, ks=ks, k=k, pt=ptB, src=src,
                                 s=(ks == 0), e=(ks == 31): nc.tensor.matmul(
                                 out=pt[:, k * 512:(k + 1) * 512],
                                 lhsT=hwb[b][:, ks * 128:(ks + 1) * 128],
                                 rhs=fm(src, ks, WF)[:, k * 512:(k + 1) * 512],
                                 start=s, stop=e),
                                 reads=[f"hwb{b}", f"{srcs}{ks}"], writes=[ptBs],
                                 tag="hmm")
                    emit("scalar", lambda oc2=oc2, pt=ptB, scol=scol, dst=dst:
                         nc.scalar.activation(out=fm(dst, oc2, WF), in_=pt[:, :],
                         func=AF.Gelu, scale=scol[:, oc2:oc2 + 1]),
                         reads=[ptBs, scs], writes=[f"{dsts}{oc2}"], tag="gev")
                    rel("B", iB)
            if hh == 0 and DEBUG:
                dma("gpsimd", dbg["f_g1"][:, :], gful[:, :], ["dbgf3"],
                    [f"gf{i}" for i in range(32)], "dO8", "dbg")
                dma("gpsimd", dbg["f_g2"][:, :], hful[:, :], ["dbgf4"],
                    [f"hf{i}" for i in range(32)], "dO9", "dbg")
            moslot = []
            for oc3 in range(8):
                b = oc3 % 2
                dma("sync", d3(hwb[b], 128), colslab(w3T, oc3 * 128, (oc3 + 1) * 128),
                    [f"hwb{b}"], ["S.w3T"], f"w_h{b}", "w3ld")
                iB = acq("B")
                ptB, ptBs = pb(iB)
                for ks in range(32):
                    for k in range(2):
                        emit("tensor", lambda b=b, ks=ks, k=k, pt=ptB, s=(ks == 0),
                             e=(ks == 31): nc.tensor.matmul(
                             out=pt[:, k * 512:(k + 1) * 512],
                             lhsT=hwb[b][:, ks * 128:(ks + 1) * 128],
                             rhs=fm(hful, ks, WF)[:, k * 512:(k + 1) * 512],
                             start=s, stop=e),
                             reads=[f"hwb{b}", f"hf{ks}"], writes=[ptBs], tag="w3mm")
                emit("scalar", lambda oc3=oc3, pt=ptB: nc.scalar.activation(
                     out=fm(gful, 16 + oc3, WF), in_=pt[:, :], func=AF.Identity,
                     scale=s3c[:, oc3:oc3 + 1]),
                     reads=[ptBs, "s3c"], writes=[f"gf{16 + oc3}"], tag="foev")
                rel("B", iB)
                moslot.append(f"gf{16 + oc3}")

            if hh == 0 and DEBUG:
                dma("gpsimd", dbg["f_fo"][:, 0:8 * WF], gful[:, 16 * WF:24 * WF],
                    ["dbgf5"], [f"gf{16 + i}" for i in range(8)], "dO10", "dbg")
            emit_ln(lambda dc: (fm(gful, 16 + dc, WF), f"gf{16 + dc}"), 8, WF,
                    lambda *a: None, bf_in=True)
            ib4 = [acq("A") for _ in range(4)]
            pb4 = [pa(i) for i in ib4]
            for k in range(2):
                pR, pRs = pb4[2 * k]
                pM, pMs = pb4[2 * k + 1]
                emit("tensor", lambda pR=pR, k=k: nc.tensor.matmul(
                     out=pR[:, :], lhsT=ones_r[0:1, :],
                     rhs=rows[0:1, 2 * 1024 + k * 512:2 * 1024 + (k + 1) * 512],
                     start=True, stop=True), reads=["row2", "ones_r"], writes=[pRs])
                emit("tensor", lambda pM=pM, k=k: nc.tensor.matmul(
                     out=pM[:, :], lhsT=ones_r[0:1, :],
                     rhs=rows[0:1, 3 * 1024 + k * 512:3 * 1024 + (k + 1) * 512],
                     start=True, stop=True), reads=["row3", "ones_r"], writes=[pMs])
            for dc in range(8):
                d = dc % 2
                dma("sync", xrot[d][:, :],
                    xsp[dc * 128:(dc + 1) * 128, hh * WF:(hh + 1) * WF],
                    [f"xrot{d}"], [], f"x_r{d}", "xfld3")
                for k in range(2):
                    pR, pRs = pb4[2 * k]
                    pM, pMs = pb4[2 * k + 1]
                    emit("vector", lambda dc=dc, k=k, pR=pR, d=d: nc.vector.tensor_mul(
                         out=t1f[d][:, :],
                         in0=fm(gful, 16 + dc, WF)[:, k * 512:(k + 1) * 512],
                         in1=pR[:, :]), reads=[f"gf{16 + dc}", pRs],
                         writes=[f"t1f{d}"])
                    emit("vector", lambda k=k, pM=pM, d=d: nc.vector.tensor_sub(
                         out=t1f[d][:, :], in0=t1f[d][:, :], in1=pM[:, :]),
                         reads=[f"t1f{d}", pMs], writes=[f"t1f{d}"])
                    emit("vector", lambda dc=dc, k=k, d=d: nc.vector.tensor_add(
                         out=xrot[d][:, k * 512:(k + 1) * 512],
                         in0=xrot[d][:, k * 512:(k + 1) * 512], in1=t1f[d][:, :]),
                         reads=[f"t1f{d}", f"xrot{d}"], writes=[f"xrot{d}"])
                dma("gpsimd",
                    xsp[dc * 128:(dc + 1) * 128, hh * WF:(hh + 1) * WF],
                    xrot[d][:, :], [f"spf{hh}"], [f"xrot{d}"], f"s_f{d}", "spillF")
            for i in ib4:
                rel("A", i)
        pr.barrier()
        pf.close()

    if DEV_STAGES >= 3:
        pa_ = ExitStack()

        def sba(name, p, f, dt):
            return sb(name, p, f, dt, pa_.enter_context)

        LNB["xht"] = [sba(f"xhtA{i}", 128, 512, BF16) for i in range(2)]
        LNB["sqt"] = [sba(f"sqtA{i}", 128, 512, BF16) for i in range(2)]
        xna = sba("xna", 128, 8 * T, BF16)
        qf = sba("qf", 128, 8 * T, BF16)
        kf = sba("kf", 128, 8 * T, BF16)
        vsb = sba("vsb", 128, 16 * 1040, BF16)
        wvsb = sba("wvsb", 128, 8 * 1024, BF16)
        sct = [sba(f"sct{i}", 128, T, BF16) for i in range(2)]
        yraw = sba("yraw", 64, T, F32)
        awb = [sba(f"awb{i}", 128, 8 * 128, BF16) for i in range(2)]
        x2c = [sba(f"x2c{i}", 128, WC, F32) for i in range(2)]

        for tt in range(16):
            emit("vector", lambda tt=tt: nc.vector.memset(
                 vsb[:, tt * 1040:(tt + 1) * 1040].rearrange(
                     "p (h o) -> p h o", o=65)[:, :, 64:65], 1.0),
                 writes=[f"vsb{tt}"], tag="vones")

        for ca in range(NCH):
            for dc in range(8):
                dma("sync", x2c[dc % 2][:, :],
                    xsp[dc * 128:(dc + 1) * 128, ca * WC:(ca + 1) * WC],
                    [f"x2c{dc % 2}"], [], f"x_c{dc % 2}", "x2ld")
                emit("vector", lambda dc=dc, ca=ca: nc.vector.tensor_copy(
                     out=fm(xna, dc, T)[:, ca * WC:(ca + 1) * WC],
                     in_=x2c[dc % 2][:, :]),
                     reads=[f"x2c{dc % 2}"], writes=[f"xna{dc}"], tag="xcast")

            def applyA(k, pR, pRs, pM, pMs, ca=ca):
                for dc in range(8):
                    d = dc % 2
                    emit("vector", lambda dc=dc, k=k, pR=pR, d=d, ca=ca:
                         nc.vector.tensor_mul(out=t1f[d][:, :],
                         in0=fm(xna, dc, T)[:, ca * WC + k * 512:ca * WC + (k + 1) * 512],
                         in1=pR[:, :]), reads=[f"xna{dc}", pRs], writes=[f"t1f{d}"])
                    emit("vector", lambda k=k, pM=pM, d=d: nc.vector.tensor_sub(
                         out=t1f[d][:, :], in0=t1f[d][:, :], in1=pM[:, :]),
                         reads=[f"t1f{d}", pMs], writes=[f"t1f{d}"])
                    emit("vector", lambda dc=dc, k=k, d=d, ca=ca: nc.vector.tensor_scalar(
                         out=fm(xna, dc, T)[:, ca * WC + k * 512:ca * WC + (k + 1) * 512],
                         in0=t1f[d][:, :], scalar1=lngc[:, dc:dc + 1],
                         scalar2=lnbc[:, dc:dc + 1], op0=ALU.mult, op1=ALU.add),
                         reads=[f"t1f{d}", "lngc", "lnbc"], writes=[f"xna{dc}"])
            emit_ln(lambda dc, ca=ca: (fm(xna, dc, T)[:, ca * WC:(ca + 1) * WC],
                    f"xna{dc}"), 8, WC, applyA, extras="fold", bf_in=True)

        if DEBUG:
            for nm, shp, dt_ in (("a_xna", [128, 8 * T], BF16),
                                 ("a_qf", [128, 8 * T], BF16),
                                 ("a_kf", [128, 8 * T], BF16),
                                 ("a_vsb", [128, 16 * 1040], BF16),
                                 ("a_y", [128, 8 * T], BF16)):
                dbg[nm] = nc.dram_tensor(nm, shp, dt_, kind="ExternalOutput").ap()
            dma("gpsimd", dbg["a_xna"][:, :], xna[:, :], ["dbgaxna"],
                [f"xna{i}" for i in range(8)], "dO11", "dbg")
        for (wsrc, wslot, dst, dslot, bcol, bslot) in (
                (wq_bf, "S.wq", qf, "qf", bqc, "bqc"),
                (wk_bf, "S.wk", kf, "kf", bkc, "bkc")):
            for oc in range(8):
                b = oc % 2
                dma("sync", d3(awb[b], 128), colslab(wsrc, oc * 128, (oc + 1) * 128),
                    [f"awb{b}"], [wslot], f"w_a{b}", "qkld")
                for k4 in range(2):
                    iB = acq("B")
                    ptB, ptBs = pb(iB)
                    for ks in range(8):
                        for k in range(2):
                            emit("tensor", lambda b=b, ks=ks, k=k, k4=k4, pt=ptB,
                                 s=(ks == 0), e=(ks == 7): nc.tensor.matmul(
                                 out=pt[:, k * 512:(k + 1) * 512],
                                 lhsT=awb[b][:, ks * 128:(ks + 1) * 128],
                                 rhs=fm(xna, ks, T)[:, k4 * 1024 + k * 512:k4 * 1024 + (k + 1) * 512],
                                 start=s, stop=e),
                                 reads=[f"awb{b}", f"xna{ks}"], writes=[ptBs],
                                 tag="qkmm")
                    emit("scalar", lambda oc=oc, k4=k4, pt=ptB, dst=dst, bcol=bcol,
                         bslot=bslot: nc.scalar.activation(
                         out=fm(dst, oc, T)[:, k4 * 1024:(k4 + 1) * 1024],
                         in_=pt[:, :], func=AF.Identity, bias=bcol[:, oc:oc + 1]),
                         reads=[ptBs, bslot], writes=[f"{dslot}{oc}"], tag="qkev")
                    rel("B", iB)

        for ks in range(8):
            dma("sync", wvsb[:, ks * 1024:(ks + 1) * 1024],
                wv_bf[ks * 128:(ks + 1) * 128, :], [f"wvsb{ks}"], ["S.wv"],
                "dPv", "wvld")
        for tt in range(16):
            iB = acq("B")
            ptB, ptBs = pb(iB)
            for ks in range(8):
                for k in range(2):
                    emit("tensor", lambda tt=tt, ks=ks, k=k, pt=ptB, s=(ks == 0):
                         nc.tensor.matmul(out=pt[:, k * 512:(k + 1) * 512],
                         lhsT=fm(xna, ks, T)[:, tt * 128:(tt + 1) * 128],
                         rhs=wvsb[:, ks * 1024 + k * 512:ks * 1024 + (k + 1) * 512],
                         start=s, stop=False),
                         reads=[f"xna{ks}", f"wvsb{ks}"], writes=[ptBs], tag="vmm")
            for k in range(2):
                emit("tensor", lambda k=k, pt=ptB: nc.tensor.matmul(
                     out=pt[:, k * 512:(k + 1) * 512], lhsT=ones_rb[0:1, :],
                     rhs=bvr[0:1, k * 512:(k + 1) * 512], start=False, stop=True),
                     reads=["ones_rb", "bvr"], writes=[ptBs], tag="vbias")
            emit("scalar", lambda tt=tt, pt=ptB: nc.scalar.activation(
                 out=vsb[:, tt * 1040:(tt + 1) * 1040].rearrange(
                     "p (h o) -> p h o", o=65)[:, :, 0:64],
                 in_=pt[:, :].rearrange("p (h o) -> p h o", o=64), func=AF.Copy),
                 reads=[ptBs], writes=[f"vsb{tt}"], tag="vev")
            rel("B", iB)

        if DEBUG:
            dma("gpsimd", dbg["a_qf"][:, :], qf[:, :], ["dbgaqf"],
                [f"qf{i}" for i in range(8)], "dO12", "dbg")
            dma("gpsimd", dbg["a_kf"][:, :], kf[:, :], ["dbgakf"],
                [f"kf{i}" for i in range(8)], "dO13", "dbg")
            dma("gpsimd", dbg["a_vsb"][:, :], vsb[:, :], ["dbgavsb"],
                [f"vsb{i}" for i in range(16)], "dO14", "dbg")
        for h in range(NH):
            oc, po = h // 2, (h % 2) * 64
            iy = [acq("A") for _ in range(4)]
            pys = [pa(i) for i in iy]
            for j in range(16):
                jb = j % 2
                iB = acq("B")
                ptB, ptBs = pb(iB)
                for u in range(2):
                    for k in range(2):
                        emit("tensor", lambda oc=oc, po=po, j=j, u=u, k=k, pt=ptB:
                             nc.tensor.matmul(
                             out=pt[:, k * 512:(k + 1) * 512],
                             lhsT=fm(kf, oc, T)[po:po + 64, j * 128:(j + 1) * 128],
                             rhs=fm(qf, oc, T)[po:po + 64, u * 1024 + k * 512:u * 1024 + (k + 1) * 512],
                             start=True, stop=True),
                             reads=[f"kf{oc}", f"qf{oc}"], writes=[ptBs], tag="qkT")
                    emit("scalar", lambda u=u, jb=jb, pt=ptB: nc.scalar.activation(
                         out=sct[jb][:, u * 1024:(u + 1) * 1024], in_=pt[:, :],
                         func=AF.Exp, scale=0.125),
                         reads=[ptBs], writes=[f"sct{jb}"], tag="sexpA")
                rel("B", iB)
                for t4 in range(4):
                    py, pys_ = pys[t4]
                    emit("tensor", lambda h=h, j=j, t4=t4, jb=jb, py=py,
                         s=(j == 0), e=(j == 15): nc.tensor.matmul(
                         out=py[0:65, :],
                         lhsT=vsb[:, j * 1040 + h * 65:j * 1040 + (h + 1) * 65],
                         rhs=sct[jb][:, t4 * 512:(t4 + 1) * 512],
                         start=s, stop=e),
                         reads=[f"vsb{j}", f"sct{jb}"], writes=[pys_], tag="avmm")
            for t4 in range(4):
                py, pys_ = pys[t4]
                emit("scalar", lambda t4=t4, py=py: nc.scalar.activation(
                     out=rows[0:1, t4 * 512:(t4 + 1) * 512],
                     in_=py[64:65, :], func=AF.Ln),
                     reads=[pys_], writes=["row0", "row1"], tag="zln")
            emit("scalar", lambda: nc.scalar.activation(
                 out=rows[0:1, 0:2048], in_=rows[0:1, 0:2048],
                 func=AF.Exp, scale=-1.0),
                 reads=["row0", "row1"], writes=["row0", "row1"], tag="zexp")
            for t4 in range(4):
                py, pys_ = pys[t4]
                emit("vector", lambda t4=t4, py=py: nc.vector.tensor_copy(
                     out=yraw[:, t4 * 512:(t4 + 1) * 512], in_=py[0:64, :]),
                     reads=[pys_], writes=["yraw"], tag="yev")
            for i in iy:
                rel("A", i)
            for t4 in range(4):
                i = acq("A")
                py, pys_ = pa(i)
                emit("tensor", lambda t4=t4, py=py: nc.tensor.matmul(
                     out=py[0:64, :], lhsT=ones_r[0:1, 0:64],
                     rhs=rows[0:1, t4 * 512:(t4 + 1) * 512],
                     start=True, stop=True),
                     reads=["row0", "row1", "ones_r"], writes=[pys_], tag="rzb")
                emit("vector", lambda h=h, oc=oc, po=po, t4=t4, py=py:
                     nc.vector.tensor_mul(
                     out=fm(qf, oc, T)[po:po + 64, t4 * 512:(t4 + 1) * 512],
                     in0=yraw[:, t4 * 512:(t4 + 1) * 512], in1=py[0:64, :]),
                     reads=["yraw", pys_], writes=[f"qf{oc}"],
                     tag="ynrm")
                rel("A", i)
        if DEBUG:
            dma("gpsimd", dbg["a_y"][:, :], qf[:, :], ["dbgay"],
                [f"qf{i}" for i in range(8)], "dO15", "dbg")
        for ca in range(NCH):
            rbA = emit_ln(lambda dc, ca=ca: (fm(qf, dc, T)[:, ca * WC:(ca + 1) * WC],
                          f"qf{dc}"), 8, WC, lambda *a: None, bf_in=True)
            RRA, REA = rbA + 2, rbA + 3
            for dc in range(8):
                d = dc % 2
                dma("sync", x2c[d][:, :],
                    xsp[dc * 128:(dc + 1) * 128, ca * WC:(ca + 1) * WC],
                    [f"x2c{d}"], [], f"x_c{d}", "x2ld2")
                iR, iM = acq("A"), acq("A")
                pR, pRs = pa(iR)
                pM, pMs = pa(iM)
                emit("tensor", lambda pR=pR: nc.tensor.matmul(
                     out=pR[:, :], lhsT=ones_r[0:1, :],
                     rhs=rows[0:1, RRA * 1024:RRA * 1024 + 512],
                     start=True, stop=True),
                     reads=[f"row{RRA}", "ones_r"], writes=[pRs])
                emit("tensor", lambda pM=pM: nc.tensor.matmul(
                     out=pM[:, :], lhsT=ones_r[0:1, :],
                     rhs=rows[0:1, REA * 1024:REA * 1024 + 512],
                     start=True, stop=True),
                     reads=[f"row{REA}", "ones_r"], writes=[pMs])
                emit("vector", lambda dc=dc, d=d, pR=pR, ca=ca: nc.vector.tensor_mul(
                     out=t1f[d][:, :],
                     in0=fm(qf, dc, T)[:, ca * WC:(ca + 1) * WC], in1=pR[:, :]),
                     reads=[f"qf{dc}", pRs], writes=[f"t1f{d}"])
                emit("vector", lambda d=d, pM=pM: nc.vector.tensor_sub(
                     out=t1f[d][:, :], in0=t1f[d][:, :], in1=pM[:, :]),
                     reads=[f"t1f{d}", pMs], writes=[f"t1f{d}"])
                emit("vector", lambda d=d: nc.vector.tensor_add(
                     out=x2c[d][:, :], in0=x2c[d][:, :], in1=t1f[d][:, :]),
                     reads=[f"t1f{d}", f"x2c{d}"], writes=[f"x2c{d}"])
                rel("A", iR)
                rel("A", iM)
                dma("gpsimd", xsp[dc * 128:(dc + 1) * 128, ca * WC:(ca + 1) * WC],
                    x2c[d][:, :], [f"spA{ca}"], [f"x2c{d}"], f"s_a{d}", "spillA")
        pr.barrier()
        pa_.close()

    if DEV_STAGES >= 4:
        pl = ExitStack()

        def sbl(name, p, f, dt):
            return sb(name, p, f, dt, pl.enter_context)

        LNB["xht"] = [sbl(f"xhtL{i}", 128, 512, BF16) for i in range(2)]
        LNB["sqt"] = [sbl(f"sqtL{i}", 128, 512, BF16) for i in range(2)]
        xnq = sbl("xnq", 128, 8 * WC, F32)
        xq = sbl("xq", 128, 8 * WC, BF16)
        xq8 = sbl("xq8", 128, 512, I8)
        h1d = [sbl(f"h1d{i}", 128, 32 * WC, BF16) for i in range(2)]
        hch = sbl("hch", 128, 32 * WC, BF16)
        fo = xq
        qwb = [sbl(f"qwb{i}", 128, 8 * 256, BF16) for i in range(2)]
        q2b = [sbl(f"q2b{i}", 128, 32 * 128, BF16) for i in range(2)]
        cmx = sbl("cmx", 128, 512, F32)
        qab = [sbl(f"qab{i}", 128, 512, F32) for i in range(2)]
        qmc = sbl("qmc", 128, 8, F32)
        onesq = sbl("onesq", 128, 128, F32)
        qsb = sbl("qsb", 128, 512, F32)
        sbc = sbl("sbc", 128, 512, F32)
        isbc = sbl("isbc", 128, 512, F32)
        hp = sbl("hp", 128, WC + 2, BF16)
        hsv = sbl("hsv", 128, 4 * 32, BF16)

        emit("vector", lambda: nc.vector.memset(onesq[:, :], 1.0), writes=["onesq"])

        def quant_act(src_fn, src_slots, ndc, qi_col, dst_fn, dst_slots, wq_col):
            for dc in range(ndc):
                d = dc % 2
                emit("scalar", lambda dc=dc, d=d: nc.scalar.activation(
                     out=qab[d][:, :], in_=src_fn(dc), func=AF.Abs),
                     reads=[src_slots(dc)], writes=[f"qab{d}"], tag="qabs")
                if dc == 0:
                    emit("vector", lambda d=d: nc.vector.tensor_copy(
                         out=cmx[:, :], in_=qab[d][:, :]),
                         reads=[f"qab{d}"], writes=["cmx"])
                else:
                    emit("vector", lambda d=d: nc.vector.tensor_max(
                         out=cmx[:, :], in0=cmx[:, :], in1=qab[d][:, :]),
                         reads=["cmx", f"qab{d}"], writes=["cmx"])
            iA = acq("A")
            pt, pts = pa(iA)
            for k in range(4):
                emit("tensor", lambda k=k, pt=pt: nc.tensor.transpose(
                     out=pt[:, k * 128:(k + 1) * 128], in_=cmx[:, k * 128:(k + 1) * 128],
                     identity=idf[:, :]), reads=["cmx", "idf"], writes=[pts],
                     tag="qT")
            for k in range(4):
                emit("vector", lambda k=k, pt=pt: nc.vector.tensor_reduce(
                     out=qmc[:, k:k + 1], in_=pt[:, k * 128:(k + 1) * 128],
                     axis=AX.X, op=ALU.max), reads=[pts], writes=["qmc"],
                     tag="qred")
            rel("A", iA)
            emit("vector", lambda: nc.vector.tensor_scalar(
                 out=qmc[:, 0:4], in0=qmc[:, 0:4], scalar1=1e-5, scalar2=1.0 / 127.0,
                 op0=ALU.max, op1=ALU.mult), reads=["qmc"], writes=["qmc"])
            emit("vector", lambda: nc.vector.reciprocal(
                 out=qmc[:, 4:8], in_=qmc[:, 0:4]), reads=["qmc"], writes=["qmc"])
            for (cbase, dsttile, dslot, wscale) in ((4, sbc, "sbc", None),
                                                    (0, isbc, "isbc", wq_col)):
                for k in range(4):
                    emit("vector", lambda k=k, cbase=cbase: nc.vector.tensor_scalar(
                         out=qsb[:, k * 128:(k + 1) * 128], in0=onesq[:, :],
                         scalar1=qmc[:, cbase + k:cbase + k + 1], scalar2=None,
                         op0=ALU.mult), reads=["qmc", "onesq"], writes=["qsb"])
                iA = acq("A")
                pt, pts = pa(iA)
                for k in range(4):
                    emit("tensor", lambda k=k, pt=pt: nc.tensor.transpose(
                         out=pt[:, k * 128:(k + 1) * 128],
                         in_=qsb[:, k * 128:(k + 1) * 128], identity=idf[:, :]),
                         reads=["qsb", "idf"], writes=[pts], tag="qbT")
                if wscale is None:
                    emit("scalar", lambda pt=pt, dsttile=dsttile: nc.scalar.activation(
                         out=dsttile[:, :], in_=pt[:, :], func=AF.Copy),
                         reads=[pts], writes=[dslot], tag="qbe")
                else:
                    emit("scalar", lambda pt=pt, dsttile=dsttile, wscale=wscale:
                         nc.scalar.activation(out=dsttile[:, :], in_=pt[:, :],
                         func=AF.Identity, scale=qsc[:, wscale:wscale + 1]),
                         reads=[pts, "qsc"], writes=[dslot], tag="qbe")
                rel("A", iA)
            for dc in range(ndc):
                d = dc % 2
                emit("vector", lambda dc=dc, d=d: nc.vector.tensor_mul(
                     out=t1f[d][:, :], in0=src_fn(dc), in1=sbc[:, :]),
                     reads=[src_slots(dc), "sbc"], writes=[f"t1f{d}"])
                emit("vector", lambda d=d: nc.vector.tensor_copy(
                     out=xq8[:, :], in_=t1f[d][:, :]),
                     reads=[f"t1f{d}"], writes=["xq8"])
                emit("scalar", lambda dc=dc: nc.scalar.activation(
                     out=dst_fn(dc), in_=xq8[:, :], func=AF.Copy),
                     reads=["xq8"], writes=[dst_slots(dc)], tag="qi8")

        def l1_tail(c, cur, prv):
            for oci in range(32):
                emit("vector", lambda oci=oci, prv=prv: nc.vector.tensor_copy(
                     out=hp[:, 1:1 + WC], in_=fm(h1d[prv], oci)),
                     reads=[f"hd{prv}_{oci}"], writes=["hp"])
                if c > 0:
                    emit("vector", lambda oci=oci, c=c: nc.vector.tensor_copy(
                         out=hp[:, 0:1], in_=hsv[:, (c - 1) * 32 + oci:(c - 1) * 32 + oci + 1]),
                         reads=[f"hsv{c - 1}"], writes=["hp"])
                else:
                    emit("vector", lambda: nc.vector.memset(hp[:, 0:1], 0.0),
                         writes=["hp"])
                if c < NCH - 1:
                    emit("vector", lambda oci=oci, cur=cur: nc.vector.tensor_copy(
                         out=hp[:, 1 + WC:2 + WC], in_=fm(h1d[cur], oci)[:, 0:1]),
                         reads=[f"hd{cur}_{oci}"], writes=["hp"])
                else:
                    emit("vector", lambda: nc.vector.memset(hp[:, 1 + WC:2 + WC], 0.0),
                         writes=["hp"])
                d = oci % 2
                emit("vector", lambda oci=oci, d=d: nc.vector.tensor_scalar(
                     out=facc2[d][:, :], in0=hp[:, 0:WC],
                     scalar1=cvw1[:, oci * 3:oci * 3 + 1], scalar2=None, op0=ALU.mult),
                     reads=["hp", "cvw1"], writes=[f"fc2{d}"])
                for k in range(1, 3):
                    emit("vector", lambda oci=oci, k=k, d=d: nc.vector.scalar_tensor_tensor(
                         out=facc2[d][:, :], in0=hp[:, k:k + WC],
                         scalar=cvw1[:, oci * 3 + k:oci * 3 + k + 1],
                         in1=facc2[d][:, :], op0=ALU.mult, op1=ALU.add),
                         reads=["hp", "cvw1", f"fc2{d}"], writes=[f"fc2{d}"])
                emit("scalar", lambda oci=oci, d=d: nc.scalar.activation(
                     out=fm(hch, oci), in_=facc2[d][:, :], func=AF.Silu,
                     bias=cvb1[:, oci:oci + 1]),
                     reads=[f"fc2{d}", "cvb1"], writes=[f"hch{oci}"], tag="cv2sl")
            quant_act(lambda dc: fm(hch, dc), lambda dc: f"hch{dc}", 32, 0,
                      lambda dc: fm(hch, dc), lambda dc: f"hch{dc}", 1)
            for oc in range(8):
                b = oc % 2
                dma("sync", d3(q2b[b], 128), colslab(q2T, oc * 128, (oc + 1) * 128),
                    [f"q2b{b}"], ["S.q2T"], f"w_2{b}", "q2ld")
                iA = acq("A")
                pt, pts = pa(iA)
                for ks in range(32):
                    emit("tensor", lambda b=b, ks=ks, pt=pt, s=(ks == 0), e=(ks == 31):
                         nc.tensor.matmul(out=pt[:, :],
                         lhsT=q2b[b][:, ks * 128:(ks + 1) * 128], rhs=fm(hch, ks),
                         start=s, stop=e),
                         reads=[f"q2b{b}", f"hch{ks}"], writes=[pts], tag="q2mm")
                emit("vector", lambda oc=oc, pt=pt: nc.vector.tensor_mul(
                     out=fm(fo, oc), in0=pt[:, :], in1=isbc[:, :]),
                     reads=[pts, "isbc"], writes=[f"xq{oc}"], tag="q2ev")
                rel("A", iA)
            rbL = emit_ln(lambda dc: (fm(fo, dc), f"xq{dc}"), 8, WC,
                          lambda *a: None, bf_in=True)
            RRL, REL = rbL + 2, rbL + 3
            for dc in range(8):
                d = dc % 2
                dma("sync", x3r[d][:, :],
                    xsp[dc * 128:(dc + 1) * 128, c * WC:(c + 1) * WC],
                    [f"x3r{d}"], [], f"x_3{d}", "x3ld2")
                iR, iM = acq("A"), acq("A")
                pR, pRs = pa(iR)
                pM, pMs = pa(iM)
                emit("tensor", lambda pR=pR, RRL=RRL: nc.tensor.matmul(
                     out=pR[:, :], lhsT=ones_r[0:1, :],
                     rhs=rows[0:1, RRL * 1024:RRL * 1024 + 512],
                     start=True, stop=True),
                     reads=[f"row{RRL}", "ones_r"], writes=[pRs])
                emit("tensor", lambda pM=pM, REL=REL: nc.tensor.matmul(
                     out=pM[:, :], lhsT=ones_r[0:1, :],
                     rhs=rows[0:1, REL * 1024:REL * 1024 + 512],
                     start=True, stop=True),
                     reads=[f"row{REL}", "ones_r"], writes=[pMs])
                emit("vector", lambda dc=dc, d=d, pR=pR: nc.vector.tensor_mul(
                     out=t1f[d][:, :], in0=fm(fo, dc), in1=pR[:, :]),
                     reads=[f"xq{dc}", pRs], writes=[f"t1f{d}"])
                emit("vector", lambda d=d, pM=pM: nc.vector.tensor_sub(
                     out=t1f[d][:, :], in0=t1f[d][:, :], in1=pM[:, :]),
                     reads=[f"t1f{d}", pMs], writes=[f"t1f{d}"])
                emit("vector", lambda d=d: nc.vector.tensor_add(
                     out=x3r[d][:, :], in0=x3r[d][:, :], in1=t1f[d][:, :]),
                     reads=[f"t1f{d}", f"x3r{d}"], writes=[f"x3r{d}"])
                rel("A", iR)
                rel("A", iM)
                dma("gpsimd", xsp[dc * 128:(dc + 1) * 128, c * WC:(c + 1) * WC],
                    x3r[d][:, :], [f"spL{c}"], [f"x3r{d}"], f"s_l{d}", "spillL")

        facc2 = [sbl(f"facc2{i}", 128, WC, F32) for i in range(2)]
        x3r = [sbl(f"x3r{i}", 128, WC, F32) for i in range(2)]

        def l1_head(c, cur):
            def get_x3(dc, c=c):
                d = dc % 2
                dma("sync", x3r[d][:, :],
                    xsp[dc * 128:(dc + 1) * 128, c * WC:(c + 1) * WC],
                    [f"x3r{d}"], [], f"x_3{d}", "x3ld")
                return (x3r[d][:, :], f"x3r{d}")
            emit_ln(get_x3, 8, WC, lambda *a: None)
            iR, iM = acq("A"), acq("A")
            pR, pRs = pa(iR)
            pM, pMs = pa(iM)
            emit("tensor", lambda pR=pR: nc.tensor.matmul(
                 out=pR[:, :], lhsT=ones_r[0:1, :],
                 rhs=rows[0:1, 2 * 1024:2 * 1024 + 512], start=True, stop=True),
                 reads=["row2", "ones_r"], writes=[pRs])
            emit("tensor", lambda pM=pM: nc.tensor.matmul(
                 out=pM[:, :], lhsT=ones_r[0:1, :],
                 rhs=rows[0:1, 3 * 1024:3 * 1024 + 512], start=True, stop=True),
                 reads=["row3", "ones_r"], writes=[pMs])
            for dc in range(8):
                d = dc % 2
                dma("sync", x3r[d][:, :],
                    xsp[dc * 128:(dc + 1) * 128, c * WC:(c + 1) * WC],
                    [f"x3r{d}"], [], f"x_3{d}", "x3ld1")
                emit("vector", lambda dc=dc, pR=pR, d=d: nc.vector.tensor_mul(
                     out=t1f[d][:, :], in0=x3r[d][:, :], in1=pR[:, :]),
                     reads=[f"x3r{d}", pRs], writes=[f"t1f{d}"])
                emit("vector", lambda dc=dc, pM=pM, d=d: nc.vector.tensor_sub(
                     out=fm(xnq, dc), in0=t1f[d][:, :], in1=pM[:, :]),
                     reads=[f"t1f{d}", pMs], writes=[f"xnq{dc}"])
            rel("A", iR)
            rel("A", iM)
            quant_act(lambda dc: fm(xnq, dc), lambda dc: f"xnq{dc}", 8, 0,
                      lambda dc: fm(xq, dc), lambda dc: f"xq{dc}", 0)
            for ocb in range(16):
                b = ocb % 2
                dma("sync", d3(qwb[b], 256), colslab(q1T, ocb * 256, (ocb + 1) * 256),
                    [f"qwb{b}"], ["S.q1T"], f"w_q{b}", "q1ld")
                for oci in range(2):
                    oc = ocb * 2 + oci
                    iA = acq("A")
                    pt, pts = pa(iA)
                    for ks in range(8):
                        emit("tensor", lambda b=b, oci=oci, ks=ks, pt=pt, s=(ks == 0),
                             e=(ks == 7): nc.tensor.matmul(out=pt[:, :],
                             lhsT=qwb[b][:, ks * 256 + oci * 128:ks * 256 + (oci + 1) * 128],
                             rhs=fm(xq, ks), start=s, stop=e),
                             reads=[f"qwb{b}", f"xq{ks}"], writes=[pts], tag="q1mm")
                    emit("vector", lambda oc=oc, pt=pt, cur=cur: nc.vector.tensor_mul(
                         out=fm(h1d[cur], oc), in0=pt[:, :], in1=isbc[:, :]),
                         reads=[pts, "isbc"], writes=[f"hd{cur}_{oc}"], tag="q1ev")
                    rel("A", iA)

        for c in range(NCH):
            l1_head(c, c % 2)
            emit("vector", lambda c=c, cur=c % 2: nc.vector.tensor_copy(
                 out=hsv[:, c * 32:(c + 1) * 32].rearrange("p (o w) -> p o w", w=1),
                 in_=h1d[cur][:, :].rearrange("p (o w) -> p o w", w=WC)[:, :, WC - 1:WC]),
                 reads=[f"hd{c % 2}_{oc}" for oc in range(32)],
                 writes=[f"hsv{c}"], tag="hsave")
            if c > 0:
                l1_tail(c - 1, c % 2, (c - 1) % 2)
        l1_tail(NCH - 1, (NCH - 1) % 2, (NCH - 1) % 2)
        pr.barrier()
        pl.close()

    FINAL_SRC = xsp

    po = ExitStack()
    oxc = sb("oxc", 128, 8 * WC, F32, po.enter_context)
    otk = [sb(f"otk{i}", 128, D, F32, po.enter_context) for i in range(2)]
    for c in range(NCH):
        dma("sync", oxc[:, :].rearrange("p (g t) -> p g t", g=8),
            FINAL_SRC.rearrange("(g p) t -> p g t", p=128)[:, :, c * WC:(c + 1) * WC],
            ["oxc"], ["odone"], "x_o", "old")
        for j in range(4):
            b = j % 2
            for dc in range(8):
                i = acq("A")
                pt, pts = pa(i)
                emit("tensor", lambda dc=dc, j=j, pt=pt: nc.tensor.transpose(
                     out=pt[:, 0:128], in_=oxc[:, dc * WC + j * 128:dc * WC + (j + 1) * 128],
                     identity=idf[:, :]), reads=["oxc", "idf"], writes=[pts],
                     tag="oT")
                emit("scalar", lambda dc=dc, b=b, pt=pt: nc.scalar.activation(
                     out=otk[b][:, dc * 128:(dc + 1) * 128], in_=pt[:, 0:128],
                     func=AF.Copy), reads=[pts], writes=[f"otk{b}"], tag="oTe")
                rel("A", i)
            dma("gpsimd", out_t[(c * 4 + j) * 128:(c * 4 + j + 1) * 128, :],
                otk[b][:, :], ["odone"], [f"otk{b}"], f"o_{b}", "ost")
    fin = emit("gpsimd", lambda: nc.gpsimd.nop(), deps=[pr.stream["gpsimd"][-1]],
               tag="final")
    fin.xwaits = tuple((s, n) for s, n in pr.dma_cnt.items()
                       if s.startswith("o_") or s.startswith("s_") or s.startswith("dO"))
    po.close()

    pr.finalize(ctx)
    with nc.Block() as block:
        pr.replay_into(block)
    ctx.close()
    return nc


def kernel(**inputs):
    nc = bass.Bass("TRN2", target_bir_lowering=False, debug=False,
                   detect_race_conditions=False)
    build(nc)
    x = np.ascontiguousarray(np.asarray(inputs["L0_attn_x"], np.float32))
    wmap = {k: np.ascontiguousarray(np.asarray(v, np.float32))
            for k, v in inputs.items() if k != "L0_attn_x"}
    in_maps = []
    for b in range(8):
        m = {"x_in": x[b]}
        m.update(wmap)
        in_maps.append(m)
    res = run_bass_kernel_spmd(nc, in_maps, core_ids=list(range(8)))
    return np.stack([np.asarray(r["out"], np.float32) for r in res.results])


# version 02 (final)
# speedup vs baseline: 1.0577x; 1.0577x over previous
import numpy as np
import concourse.bass as bass
import concourse.mybir as mybir
from concourse.bass_utils import run_bass_kernel_spmd
from contextlib import ExitStack

F32 = mybir.dt.float32
BF16 = mybir.dt.bfloat16
I8 = mybir.dt.int8
I32 = mybir.dt.int32
AF = mybir.ActivationFunctionType
ALU = mybir.AluOpType
AX = mybir.AxisListType

ENGS = ("sync", "gpsimd", "vector", "scalar", "tensor")

T = 2048
D = 1024
DI = 2048
DS = 16
DTR = 64
DHID = 4096
NH = 16
DH = 64
EPS = 1e-5
WC = 512
NCH = T // WC
WF = 1024
NHF = T // WF

DEV_STAGES = 4
DEBUG = False
DEBUG_CHUNK = 0
MB = 0
DEBUG_BARRIER = False


class Op:
    __slots__ = ("eng", "fn", "deps", "need_inc", "inc_val", "dma", "dma_val",
                 "tag", "xwaits", "need_drain")


class Prog:
    def __init__(self, nc):
        self.nc = nc
        self.ops = []
        self.stream = {e: [] for e in ENGS}
        self.wr = {}
        self.rd = {}
        self.dma_cnt = {}
        self.sems = {}
        self.total_sems = set()

    def q(self, eng):
        return getattr(self.nc, eng)

    def emit(self, eng, fn, reads=(), writes=(), dma=None, deps=(), tag=""):
        op = Op()
        op.eng, op.fn, op.tag = eng, fn, tag
        op.need_inc, op.inc_val, op.dma = False, None, dma
        op.xwaits = ()
        op.need_drain = False
        if dma is not None:
            self.dma_cnt[dma] = self.dma_cnt.get(dma, 0) + 16
            op.dma_val = self.dma_cnt[dma]
        dep = set(deps)
        for s in reads:
            w = self.wr.get(s)
            if w is not None:
                dep.add(w)
        for s in writes:
            w = self.wr.get(s)
            if w is not None:
                dep.add(w)
            for r in self.rd.get(s, ()):
                dep.add(r)
        op.deps = [d for d in dep if d.eng != eng or d.dma is not None]
        if eng in ("vector", "scalar", "gpsimd") and dma is None:
            recent = self.stream[eng][-8:]
            for d in dep:
                if d.eng == eng and d.dma is None and d in recent:
                    op.need_drain = True
                    break
        self.ops.append(op)
        self.stream[eng].append(op)
        for s in reads:
            self.rd.setdefault(s, []).append(op)
        for s in writes:
            self.wr[s] = op
            self.rd[s] = []
        return op

    def barrier(self, with_dma=True):
        lasts = [self.stream[e][-1] for e in ENGS if self.stream[e]]
        nc = self.nc
        fns = {
            "sync": lambda: nc.sync.nop(),
            "tensor": lambda: nc.tensor.nop(),
            "vector": lambda: nc.vector.memset(self.fence_col[:, 0:1], 0.0),
            "scalar": lambda: nc.scalar.activation(
                out=self.fence_col[:, 1:2], in_=self.fence_col[:, 1:2],
                func=mybir.ActivationFunctionType.Identity),
            "gpsimd": lambda: nc.gpsimd.memset(self.fence_col[:, 2:3], 0.0),
        }
        xw = tuple(self.dma_cnt.items()) if with_dma else ()
        for e in ENGS:
            op = self.emit(e, fns[e], deps=lasts, tag="barrier")
            op.xwaits = xw

    def finalize(self, ctx):
        for op in self.ops:
            for d in op.deps:
                if d.dma is None:
                    d.need_inc = True
                elif d.dma in self.total_sems:
                    d.dma_val = self.dma_cnt[d.dma]
        for e in ENGS:
            cnt = 0
            for op in self.stream[e]:
                if op.dma is None and op.need_inc:
                    cnt += 1
                    op.inc_val = cnt
        names = set(self.dma_cnt)
        for e in ENGS:
            if any(op.dma is None and op.need_inc for op in self.stream[e]):
                names.add("eng_" + e)
        for n in sorted(names):
            self.sems[n] = ctx.enter_context(self.nc.semaphore(n))

    def replay_into(self, block):
        for ename in ENGS:
            ops = self.stream[ename]
            if not ops:
                continue

            def runner(_eng, ops=ops, ename=ename):
                hw = {}
                for op in ops:
                    waits = dict(op.xwaits)
                    for d in op.deps:
                        if d.dma is not None:
                            k, v = d.dma, d.dma_val
                        else:
                            k, v = "eng_" + d.eng, d.inc_val
                        if waits.get(k, 0) < v:
                            waits[k] = v
                    for k, v in sorted(waits.items()):
                        if hw.get(k, 0) >= v:
                            continue
                        hw[k] = v
                        self.q(ename).wait_ge(self.sems[k], v)
                    if op.need_drain:
                        self.q(ename).drain()
                    inst = op.fn()
                    if op.dma is not None:
                        inst.then_inc(self.sems[op.dma], 16)
                    elif op.need_inc:
                        inst.then_inc(self.sems["eng_" + ename], 1)
            getattr(block, ename)(runner)


def rearr_cols(ap, p=128):
    return ap.rearrange("(g p) s -> p g s", p=p)


def rearr_vec(ap, p=128):
    return ap.rearrange("(g p) -> p g", p=p)


def colslab(ap, c0, c1, p=128):
    return ap[:, c0:c1].rearrange("(g p) n -> p g n", p=p)


def d3(t, n):
    return t[:, :].rearrange("p (g n) -> p g n", n=n)


def build(nc):
    pr = Prog(nc)
    pr.total_sems.update({"dP", "dPg", "dPv"})
    pr.total_sems.update({"c_S.ip", "c_S.xp", "c_S.dt", "c_S.op", "c_S.w1c",
                          "c_S.w2c", "c_S.h1c", "c_S.h2c", "c_S.w3c", "c_S.wq",
                          "c_S.wk", "c_S.wv"})
    emit = pr.emit

    x_in = nc.dram_tensor("x_in", [T, D], F32, kind="ExternalInput").ap()
    out_t = nc.dram_tensor("out", [T, D], F32, kind="ExternalOutput").ap()
    W = {}
    wshapes = {
        "L0_attn_norm_w": [D], "L0_attn_in_proj_w": [D, 2 * DI],
        "L0_attn_conv_w": [DI, 4], "L0_attn_conv_b": [DI],
        "L0_attn_x_proj_w": [DI, 96], "L0_attn_dt_proj_w": [DTR, DI],
        "L0_attn_dt_proj_b": [DI], "L0_attn_A_log": [DI, DS], "L0_attn_D": [DI],
        "L0_attn_out_proj_w": [DI, D],
        "L0_ffn_w1_w": [DHID, D], "L0_ffn_w1_s": [DHID],
        "L0_ffn_w2_w": [DHID, D], "L0_ffn_w2_s": [DHID],
        "L0_ffn_h1_w": [DHID, DHID], "L0_ffn_h1_s": [DHID],
        "L0_ffn_h2_w": [DHID, DHID], "L0_ffn_h2_s": [DHID],
        "L0_ffn_w3_w": [D, DHID], "L0_ffn_w3_s": [D],
        "L1_attn_ln_g": [D], "L1_attn_ln_b": [D],
        "L1_attn_Wq": [D, D], "L1_attn_bq": [D],
        "L1_attn_Wk": [D, D], "L1_attn_bk": [D],
        "L1_attn_Wv": [D, D], "L1_attn_bv": [D],
        "L1_ffn_w1": [DHID, D], "L1_ffn_conv_w": [DHID, 1, 3],
        "L1_ffn_conv_b": [DHID], "L1_ffn_w2": [D, DHID],
    }
    for n, s in wshapes.items():
        W[n] = nc.dram_tensor(n, s, F32, kind="ExternalInput").ap()

    def dram(name, shape):
        return nc.dram_tensor(name, shape, BF16, kind="Internal").ap()

    ipw_bf = dram("ipw_bf", [D, 2 * DI])
    xpw_bf = dram("xpw_bf", [DI, 96])
    dtw_bf = dram("dtw_bf", [DTR, DI])
    opw_bf = dram("opw_bf", [DI, D])
    wq_bf = dram("wq_bf", [D, D])
    wk_bf = dram("wk_bf", [D, D])
    wv_bf = dram("wv_bf", [D, D])
    w1c = dram("w1c", [DHID, D])
    w2c = dram("w2c", [DHID, D])
    h1c = dram("h1c", [DHID, DHID])
    h2c = dram("h2c", [DHID, DHID])
    w3c = dram("w3c", [D, DHID])
    q1c = dram("q1c", [DHID, D])
    q2c = dram("q2c", [D, DHID])
    w1T = dram("w1T", [D, DHID])
    w2T = dram("w2T", [D, DHID])
    h1T = dram("h1T", [DHID, DHID])
    h2T = dram("h2T", [DHID, DHID])
    w3T = dram("w3T", [DHID, D])
    q1T = dram("q1T", [D, DHID])
    q2T = dram("q2T", [DHID, D])
    xsp = nc.dram_tensor("xsp", [D, T], F32, kind="Internal").ap()

    ctx = ExitStack()
    S = ctx.enter_context

    def sb(name, p, f, dt, st=S):
        return st(nc.sbuf_tensor(name, [p, f], dt))

    psA = [S(nc.psum_tensor(f"PA{i}", [128, 512], F32)) for i in range(4)]
    psB = [S(nc.psum_tensor(f"PB{i}", [128, 1024], F32)) for i in range(2)]
    busy = {"A": [False] * 4, "B": [False] * 2}

    def acq(pool):
        for i, b in enumerate(busy[pool]):
            if not b:
                busy[pool][i] = True
                return i
        raise RuntimeError(f"psum pool {pool} exhausted")

    def rel(pool, i):
        busy[pool][i] = False

    def pa(i):
        return psA[i], f"PA{i}"

    def pb(i):
        return psB[i], f"PB{i}"

    pr.fence_col = sb("fencec", 128, 4, F32)
    idf = sb("idf", 128, 128, F32)
    idi = sb("idi", 128, 128, I32)
    ones_r = sb("ones_r", 1, 128, F32)
    ones_rb = sb("ones_rb", 1, 128, BF16)
    ones_cb = sb("ones_cb", 128, 1, BF16)
    ones_c = sb("ones_c", 128, 1, F32)
    pat = sb("pat", 16, 16 * 128, BF16)
    acols = sb("acols", 128, 16 * DS, F32)
    convw = sb("convw", 128, 16 * 4, F32)
    convb = sb("convb", 128, 16, F32)
    dcol = sb("dcol", 128, 16, F32)
    dtbc = sb("dtbc", 128, 16, F32)
    nrmw = sb("nrmw", 128, 8, F32)
    lngc = sb("lngc", 128, 8, F32)
    lnbc = sb("lnbc", 128, 8, F32)
    bqc = sb("bqc", 128, 8, F32)
    bkc = sb("bkc", 128, 8, F32)
    bvr = sb("bvr", 1, D, BF16)
    s1c = sb("s1c", 128, 32, F32)
    s2c = sb("s2c", 128, 32, F32)
    h1sc = sb("h1sc", 128, 32, F32)
    h2sc = sb("h2sc", 128, 32, F32)
    s3c = sb("s3c", 128, 8, F32)
    cvw1 = sb("cvw1", 128, 32 * 3, F32)
    cvb1 = sb("cvb1", 128, 32, F32)
    qsc = sb("qsc", 128, 4, F32)
    rows = sb("rows", 1, 5 * 1024, F32)
    rowq = sb("rowq", 1, 8, F32)
    t1f = [sb(f"t1f{i}", 128, 512, F32) for i in range(2)]
    LNB = {}

    def row(i, w=512):
        return rows[0:1, i * 1024:i * 1024 + w]

    def dma(eng, out_ap, in_ap, wslots, rslots, sem, tag=""):
        return emit(eng, lambda o=out_ap, i=in_ap, e=eng: pr.q(e).dma_start(out=o, in_=i),
                    reads=rslots, writes=wslots, dma=sem, tag=tag)

    def dmaT(out_ap, in_ap, wslots, rslots, sem="dT", tag=""):
        return emit("sync", lambda o=out_ap, i=in_ap: nc.sync.dma_start_transpose(out=o, in_=i),
                    reads=rslots, writes=wslots, dma=sem, tag=tag)

    cpool = sb("cpool", 128, 8, F32)
    emit("vector", lambda: nc.vector.memset(pr.fence_col[:, :], 0.0),
         writes=["fencec"])
    cvals = [1e-5, 1.0, -0.5, 0.125, -1.0, 0.0, 0.5, 2.0]
    for i, v in enumerate(cvals):
        emit("vector", lambda i=i, v=v: nc.vector.memset(cpool[:, i:i + 1], v),
             writes=["cpool"])
        nc.const_aps.register(F32, v, cpool[:, i:i + 1])
    cfence = pr.wr["cpool"]
    emit("scalar", lambda: nc.scalar.nop(), deps=[cfence], tag="cfence")
    emit("gpsimd", lambda: nc.gpsimd.iota(idi[:, :], pattern=[[1, 128]], base=0,
         channel_multiplier=-1), writes=["idi"])
    emit("vector", lambda: nc.vector.tensor_single_scalar(out=idf[:, :], in_=idi[:, :],
         scalar=0, op=ALU.is_equal), reads=["idi"], writes=["idf"])
    for t_, sl in ((ones_r, "ones_r"), (ones_rb, "ones_rb"), (ones_cb, "ones_cb"),
                   (ones_c, "ones_c")):
        emit("vector", lambda t_=t_: nc.vector.memset(t_[:, :], 1.0), writes=[sl])
    emit("gpsimd", lambda: nc.gpsimd.iota(
         pat[:, :], pattern=[[1, 16], [0, 128]], base=0, channel_multiplier=-1,
         allow_small_or_imprecise_dtypes=True), writes=["pat"])
    emit("vector", lambda: nc.vector.tensor_single_scalar(
         out=pat[:, :], in_=pat[:, :], scalar=0, op=ALU.is_equal),
         reads=["pat"], writes=["pat"])

    vld = sb("vld", 32, 128, F32)

    def load_cols(dst, vec, G, slot):
        dma("sync", vld[0:G, :], vec.rearrange("(g n) -> g n", n=128),
            ["vld"], [], "dV", "vl")
        i = acq("A")
        pt, pts = pa(i)
        emit("tensor", lambda G=G, pt=pt: nc.tensor.transpose(
             out=pt[:, 0:G], in_=vld[0:G, :], identity=idf[0:G, 0:G]),
             reads=["vld", "idf"], writes=[pts], tag="vlT")
        emit("scalar", lambda G=G, pt=pt, dst=dst: nc.scalar.activation(
             out=dst[:, 0:G], in_=pt[:, 0:G], func=AF.Copy),
             reads=[pts], writes=[slot, "vld"], tag="vlE")
        rel("A", i)

    dma("sync", d3(acols, DS), rearr_cols(W["L0_attn_A_log"]), ["acols"], [], "dP")
    dma("sync", d3(convw, 4), rearr_cols(W["L0_attn_conv_w"]), ["convw"], [], "dP")
    load_cols(convb, W["L0_attn_conv_b"], 16, "convb")
    load_cols(dcol, W["L0_attn_D"], 16, "dcol")
    load_cols(dtbc, W["L0_attn_dt_proj_b"], 16, "dtbc")
    load_cols(nrmw, W["L0_attn_norm_w"], 8, "nrmw")
    load_cols(lngc, W["L1_attn_ln_g"], 8, "lngc")
    load_cols(lnbc, W["L1_attn_ln_b"], 8, "lnbc")
    load_cols(bqc, W["L1_attn_bq"], 8, "bqc")
    load_cols(bkc, W["L1_attn_bk"], 8, "bkc")
    load_cols(s1c, W["L0_ffn_w1_s"], 32, "s1c")
    load_cols(s2c, W["L0_ffn_w2_s"], 32, "s2c")
    load_cols(h1sc, W["L0_ffn_h1_s"], 32, "h1sc")
    load_cols(h2sc, W["L0_ffn_h2_s"], 32, "h2sc")
    load_cols(s3c, W["L0_ffn_w3_s"], 8, "s3c")
    dma("sync", d3(cvw1, 3),
        rearr_cols(W["L1_ffn_conv_w"].rearrange("h one k -> (h one) k")),
        ["cvw1"], [], "dP")
    load_cols(cvb1, W["L1_ffn_conv_b"], 32, "cvb1")
    dma("gpsimd", bvr[0:1, :], W["L1_attn_bv"].rearrange("(one d) -> one d", one=1), ["bvr"], [], "dPg")

    emit("scalar", lambda: nc.scalar.activation(out=acols[:, :], in_=acols[:, :],
         func=AF.Exp), reads=["acols"], writes=["acols"])
    emit("vector", lambda: nc.vector.tensor_scalar(out=acols[:, :], in0=acols[:, :],
         scalar1=-1.0, scalar2=None, op0=ALU.mult), reads=["acols"], writes=["acols"])

    stg = ExitStack()
    tb = [sb(f"tb{i}", 128, 1024, BF16, stg.enter_context) for i in range(2)]
    wsl = sb("wsl", 128, 1024, F32, stg.enter_context)
    wsl8 = sb("wsl8", 128, 1024, I8, stg.enter_context)
    wslb = sb("wslb", 128, 1024, BF16, stg.enter_context)
    wacc = sb("wacc", 128, 4, F32, stg.enter_context)

    def stage_cast(dst, src, slot):
        dma("gpsimd", dst[:, :], src, [slot], [], "c_" + slot, "cast")

    def stage_T(dstT, srcC, rn, cn, slot_src, slot_dst):
        for k in range(cn // 128):
            for r0 in range(0, rn, 1024):
                b = (k * ((rn + 1023) // 1024) + r0 // 1024) % 2
                rw = min(1024, rn - r0)
                dmaT(tb[b][:, 0:rw], srcC[r0:r0 + rw, k * 128:(k + 1) * 128],
                     [f"tb{b}"], [slot_src], f"dT{b}", "sT")
                dma("sync", dstT[k * 128:(k + 1) * 128, r0:r0 + rw], tb[b][:, 0:rw],
                    [slot_dst], [f"tb{b}"], f"dT{b}", "sTo")

    stage_cast(ipw_bf, W["L0_attn_in_proj_w"], "S.ip")
    stage_cast(xpw_bf, W["L0_attn_x_proj_w"], "S.xp")
    stage_cast(dtw_bf, W["L0_attn_dt_proj_w"], "S.dt")
    stage_cast(opw_bf, W["L0_attn_out_proj_w"], "S.op")
    stage_cast(w1c, W["L0_ffn_w1_w"], "S.w1c")
    stage_cast(w2c, W["L0_ffn_w2_w"], "S.w2c")
    stage_cast(h1c, W["L0_ffn_h1_w"], "S.h1c")
    stage_cast(h2c, W["L0_ffn_h2_w"], "S.h2c")
    stage_cast(w3c, W["L0_ffn_w3_w"], "S.w3c")
    stage_cast(wq_bf, W["L1_attn_Wq"], "S.wq")
    stage_cast(wk_bf, W["L1_attn_Wk"], "S.wk")
    stage_cast(wv_bf, W["L1_attn_Wv"], "S.wv")
    stage_T(w1T, w1c, DHID, D, "S.w1c", "S.w1T")
    stage_T(w2T, w2c, DHID, D, "S.w2c", "S.w2T")
    stage_T(h1T, h1c, DHID, DHID, "S.h1c", "S.h1T")
    stage_T(h2T, h2c, DHID, DHID, "S.h2c", "S.h2T")
    stage_T(w3T, w3c, D, DHID, "S.w3c", "S.w3T")

    def quant_weight(wsrc, rn, cn, dstC, qi, slot):
        nsl = rn // 128
        ncc = cn // 1024
        emit("vector", lambda: nc.vector.memset(wacc[:, qi:qi + 1], 0.0),
             writes=["wacc"])
        for r in range(nsl):
          for cc in range(ncc):
            dma("sync", wsl[:, :], wsrc[r * 128:(r + 1) * 128, cc * 1024:(cc + 1) * 1024],
                ["wsl"], [], "dL", "qw")
            emit("scalar", lambda: nc.scalar.activation(
                 out=wsl[:, :], in_=wsl[:, :], func=AF.Abs),
                 reads=["wsl"], writes=["wsl"])
            emit("vector", lambda: nc.vector.reduce_sum(
                 out=wacc[:, qi + 2:qi + 3], in_=wsl[:, :], axis=AX.X),
                 reads=["wsl"], writes=["wacc"])
            emit("vector", lambda: nc.vector.tensor_add(
                 out=wacc[:, qi:qi + 1], in0=wacc[:, qi:qi + 1],
                 in1=wacc[:, qi + 2:qi + 3]), reads=["wacc"], writes=["wacc", "wsl"])
        i = acq("A")
        pt, pts = pa(i)
        emit("tensor", lambda pt=pt: nc.tensor.matmul(
             out=pt[0:1, 0:1], lhsT=ones_c[:, :], rhs=wacc[:, qi:qi + 1],
             start=True, stop=True), reads=["wacc", "ones_c"], writes=[pts])
        emit("vector", lambda pt=pt, rn=rn, cn=cn: nc.vector.tensor_scalar(
             out=rowq[0:1, qi:qi + 1], in0=pt[0:1, 0:1],
             scalar1=1.0 / (rn * cn), scalar2=None, op0=ALU.mult),
             reads=[pts], writes=["row15"])
        rel("A", i)
        emit("vector", lambda: nc.vector.reciprocal(
             out=rowq[0:1, 2 + qi:3 + qi], in_=rowq[0:1, qi:qi + 1]),
             reads=["row15"], writes=["row15"])
        i = acq("A")
        pt, pts = pa(i)
        emit("tensor", lambda pt=pt: nc.tensor.matmul(
             out=pt[:, 0:4], lhsT=ones_r[0:1, :],
             rhs=rowq[0:1, 0:4], start=True, stop=True),
             reads=["row15", "ones_r"], writes=[pts])
        emit("vector", lambda pt=pt: nc.vector.tensor_copy(
             out=qsc[:, :], in_=pt[:, 0:4]), reads=[pts], writes=["qsc"])
        rel("A", i)
        for r in range(nsl):
          for cc in range(ncc):
            dma("sync", wsl[:, :], wsrc[r * 128:(r + 1) * 128, cc * 1024:(cc + 1) * 1024],
                ["wsl"], [], "dL", "qw2")
            emit("vector", lambda: nc.vector.tensor_scalar(
                 out=wsl[:, :], in0=wsl[:, :], scalar1=qsc[:, 2 + qi:3 + qi],
                 scalar2=None, op0=ALU.mult), reads=["wsl", "qsc"], writes=["wsl"])
            emit("vector", lambda: nc.vector.tensor_copy(
                 out=wsl8[:, :], in_=wsl[:, :]), reads=["wsl"], writes=["wsl8"])
            emit("vector", lambda: nc.vector.tensor_scalar(
                 out=wslb[:, :], in0=wsl8[:, :], scalar1=-1.0, scalar2=1.0,
                 op0=ALU.max, op1=ALU.min), reads=["wsl8"], writes=["wslb"])
            dma("gpsimd", dstC[r * 128:(r + 1) * 128, cc * 1024:(cc + 1) * 1024],
                wslb[:, :], [slot], ["wslb"], "dQ", "qst")

    emit("vector", lambda: nc.vector.memset(rowq[0:1, :], 1.0), writes=["row15"])
    quant_weight(W["L1_ffn_w1"], DHID, D, q1c, 0, "S.q1c")
    quant_weight(W["L1_ffn_w2"], D, DHID, q2c, 1, "S.q2c")
    stage_T(q1T, q1c, DHID, D, "S.q1c", "S.q1T")
    stage_T(q2T, q2c, D, DHID, "S.q2c", "S.q2T")

    def emit_ln(get_tile, ndc, wlen, apply_cb, extras=None, bf_in=False):
        rb = 0
        pool = "B" if wlen > 512 else "A"
        iS1, iS2 = acq(pool), acq(pool)
        pp = pb if pool == "B" else pa
        p1, p1s = pp(iS1)
        p2, p2s = pp(iS2)
        nk = wlen // 512
        for dc in range(ndc):
            apx, slot = get_tile(dc)
            d = dc % 2
            xht, sqt = LNB["xht"], LNB["sqt"]
            if bf_in:
                hs = slot
                hap = apx
            else:
                hs = f"xht{d}"
                hap = xht[d][:, 0:wlen]
                emit("scalar", lambda apx=apx, d=d, wlen=wlen, xht=xht:
                     nc.scalar.activation(
                     out=xht[d][:, 0:wlen], in_=apx, func=AF.Copy),
                     reads=[slot], writes=[hs], tag="lncp")
            emit("vector", lambda hap=hap, d=d, wlen=wlen, sqt=sqt:
                 nc.vector.tensor_mul(
                 out=sqt[d][:, 0:wlen], in0=hap, in1=hap),
                 reads=[hs], writes=[f"sqt{d}"], tag="lnsq")
            for k in range(nk):
                kk = k * 512 if nk > 1 else 0
                emit("tensor", lambda hap=hap, p1=p1, k=k, s=(dc == 0), e=(dc == ndc - 1):
                     nc.tensor.matmul(out=p1[0:1, k * 512:(k + 1) * 512],
                     lhsT=ones_cb[:, :], rhs=hap[:, k * 512:(k + 1) * 512],
                     start=s, stop=e), reads=[hs, "ones_cb"], writes=[p1s], tag="lnm1")
                emit("tensor", lambda d=d, p2=p2, k=k, s=(dc == 0), e=(dc == ndc - 1),
                     sqt=sqt: nc.tensor.matmul(out=p2[0:1, k * 512:(k + 1) * 512],
                     lhsT=ones_cb[:, :], rhs=sqt[d][:, k * 512:(k + 1) * 512],
                     start=s, stop=e), reads=[f"sqt{d}", "ones_cb"], writes=[p2s],
                     tag="lnm2")
        RM, RV, RR, RE, RQ = rb, rb + 1, rb + 2, rb + 3, rb + 4
        rmu, rv, rr, rmr = (f"row{RM}", f"row{RV}", f"row{RR}", f"row{RE}")
        emit("vector", lambda p1=p1, wlen=wlen: nc.vector.tensor_scalar(
             out=row(RM, wlen), in0=p1[0:1, 0:wlen], scalar1=1.0 / D, scalar2=None,
             op0=ALU.mult), reads=[p1s], writes=[rmu])
        rel(pool, iS1)
        emit("vector", lambda p2=p2, wlen=wlen: nc.vector.tensor_scalar(
             out=row(RV, wlen), in0=p2[0:1, 0:wlen], scalar1=1.0 / D, scalar2=None,
             op0=ALU.mult), reads=[p2s], writes=[rv])
        rel(pool, iS2)
        emit("vector", lambda wlen=wlen: nc.vector.tensor_mul(
             out=row(RR, wlen), in0=row(RM, wlen), in1=row(RM, wlen)),
             reads=[rmu], writes=[rr])
        emit("vector", lambda wlen=wlen: nc.vector.tensor_sub(
             out=row(RV, wlen), in0=row(RV, wlen), in1=row(RR, wlen)),
             reads=[rv, rr], writes=[rv])
        emit("scalar", lambda wlen=wlen: nc.scalar.activation(
             out=row(RR, wlen), in_=row(RV, wlen), func=AF.Ln, bias=EPS),
             reads=[rv], writes=[rr])
        emit("scalar", lambda wlen=wlen: nc.scalar.activation(
             out=row(RR, wlen), in_=row(RR, wlen), func=AF.Exp, scale=-0.5),
             reads=[rr], writes=[rr])
        emit("vector", lambda wlen=wlen: nc.vector.tensor_mul(
             out=row(RE, wlen), in0=row(RM, wlen), in1=row(RR, wlen)),
             reads=[rmu, rr], writes=[rmr])
        if extras == "fold":
            rq = f"row{RQ}"
            emit("vector", lambda wlen=wlen: nc.vector.tensor_mul(
                 out=row(RQ, wlen), in0=row(RV, wlen), in1=row(RR, wlen)),
                 reads=[rv, rr], writes=[rq])
            emit("vector", lambda wlen=wlen: nc.vector.tensor_mul(
                 out=row(RQ, wlen), in0=row(RQ, wlen), in1=row(RR, wlen)),
                 reads=[rq, rr], writes=[rq])
            emit("scalar", lambda wlen=wlen: nc.scalar.activation(
                 out=row(RQ, wlen), in_=row(RQ, wlen), func=AF.Ln, bias=EPS),
                 reads=[rq], writes=[rq])
            emit("scalar", lambda wlen=wlen: nc.scalar.activation(
                 out=row(RQ, wlen), in_=row(RQ, wlen), func=AF.Exp, scale=-0.5),
                 reads=[rq], writes=[rq])
            emit("vector", lambda wlen=wlen: nc.vector.tensor_mul(
                 out=row(RR, wlen), in0=row(RR, wlen), in1=row(RQ, wlen)),
                 reads=[rr, rq], writes=[rr])
            emit("vector", lambda wlen=wlen: nc.vector.tensor_mul(
                 out=row(RE, wlen), in0=row(RE, wlen), in1=row(RQ, wlen)),
                 reads=[rmr, rq], writes=[rmr])
        for k in range(wlen // 512):
            iR, iM = acq("A"), acq("A")
            pR, pRs = pa(iR)
            pM, pMs = pa(iM)
            emit("tensor", lambda pR=pR, k=k: nc.tensor.matmul(
                 out=pR[:, :], lhsT=ones_r[0:1, :],
                 rhs=rows[0:1, RR * 1024 + k * 512:RR * 1024 + (k + 1) * 512],
                 start=True, stop=True), reads=[rr, "ones_r"], writes=[pRs],
                 tag="lnbr")
            emit("tensor", lambda pM=pM, k=k: nc.tensor.matmul(
                 out=pM[:, :], lhsT=ones_r[0:1, :],
                 rhs=rows[0:1, RE * 1024 + k * 512:RE * 1024 + (k + 1) * 512],
                 start=True, stop=True), reads=[rmr, "ones_r"], writes=[pMs],
                 tag="lnbm")
            apply_cb(k, pR, pRs, pM, pMs)
            rel("A", iR)
            rel("A", iM)
        return rb

    pm = ExitStack()

    def sbm(name, p, f, dt):
        return sb(name, p, f, dt, pm.enter_context)

    LNB["xht"] = [sbm(f"xhtM{i}", 128, 512, BF16) for i in range(2)]
    LNB["sqt"] = [sbm(f"sqtM{i}", 128, 512, BF16) for i in range(2)]
    xtok0 = sbm("xtok0", 128, D, F32)
    xtok = [xtok0, xtok0]
    xc = sbm("xc", 128, 8 * WC, F32)
    xnm = sbm("xnm", 128, 8 * WC, BF16)
    ipwb = [sbm(f"ipwb{i}", 128, 8 * 128, BF16) for i in range(2)]
    upad = sbm("upad", 128, 16 * (WC + 3), BF16)
    srmo = sbm("srmo", 128, 16 * WC, BF16)
    ucv = sbm("ucv", 128, 16 * WC, BF16)
    delta = sbm("delta", 128, 16 * WC, BF16)
    duv = sbm("duv", 128, 16 * WC, BF16)
    yac2 = sbm("yac2", 128, 8 * WC, BF16)
    yac1 = sbm("yac1", 128, 8 * WC, BF16)
    dtq = sbm("dtq", 64, WC, BF16)
    brow = sbm("brow", 16, WC, BF16)
    crow = sbm("crow", 16, WC, BF16)
    xpwsb = sbm("xpwsb", 128, 16 * 96, BF16)
    dtwsb = sbm("dtwsb", 64, DI, BF16)
    opwb0 = sbm("opwb0", 128, 16 * 128, BF16)
    opwb = [opwb0, opwb0]
    hcar = sbm("hcar", 128, 256, F32)
    dma("sync", d3(xpwsb, 96), rearr_cols(xpw_bf), ["xpwsb"], ["S.xp"], "d_xpw")
    dma("sync", dtwsb[:, :], dtw_bf[:, :], ["dtwsb"], ["S.dt"], "d_dtw")
    bexs = sbm("bexs", 128, WC, BF16)
    cexs = sbm("cexs", 128, WC, BF16)
    abuf = [sbm(f"abuf{i}", 128, WC, F32) for i in range(2)]
    bbuf = [sbm(f"bbuf{i}", 128, WC, BF16) for i in range(2)]
    hbuf = [sbm(f"hbuf{i}", 128, WC, BF16) for i in range(2)]
    tcb = [sbm(f"tcb{i}", 128, WC, BF16) for i in range(2)]
    facc = [sbm("facc0", 128, WC, F32)]
    facc.append(facc[0])

    emit("vector", lambda: nc.vector.memset(hcar[:, :], 0.0), writes=["hcar"])
    for g in range(16):
        emit("vector", lambda g=g: nc.vector.memset(
             upad[:, g * (WC + 3):g * (WC + 3) + 3], 0.0), writes=[f"up{g}"])

    def upg(g):
        return upad[:, g * (WC + 3):(g + 1) * (WC + 3)]

    def fm(t, i, w=WC):
        return t[:, i * w:(i + 1) * w]

    def yact(g):
        return fm(yac1, g) if g < 8 else fm(yac2, g - 8)

    def yslot(g):
        return f"yac1{g}" if g < 8 else f"yac2{g - 8}"

    dbg = {}
    if DEBUG:
        for nm, shp, dt_ in (("d_xc", [128, 8 * WC], F32), ("d_xnm", [128, 8 * WC], BF16),
                             ("d_u", [128, 16 * (WC + 3)], BF16),
                             ("d_sr", [128, 16 * WC], BF16), ("d_uc", [128, 16 * WC], BF16),
                             ("d_dtq", [64, WC], BF16), ("d_brow", [16, WC], BF16),
                             ("d_crow", [16, WC], BF16), ("d_delta", [128, 16 * WC], BF16),
                             ("d_y", [128, 16 * WC], BF16), ("d_yf", [128, 16 * WC], BF16),
                             ("d_mo", [128, 8 * WC], BF16),
                             ("d_opw", [128, 16 * 128], BF16),
                             ("d_x1", [128, 8 * WC], F32)):
            dbg[nm] = nc.dram_tensor(nm, shp, dt_, kind="ExternalOutput").ap()

    def dump(nm, ap, rslots):
        if DEBUG:
            dma("gpsimd", dbg[nm][:, :], ap, [f"dbg{nm}"], rslots, "dO_" + nm, "dbg")

    for c in range(NCH):
        if DEBUG_BARRIER:
            pr.barrier(with_dma=(DEBUG_BARRIER == 1))
        for j in range(4):
            jj = j % 2
            dma("sync", xtok[jj][:, :], x_in[(c * 4 + j) * 128:(c * 4 + j + 1) * 128, :],
                ["xtok0"], [], "x_t0")
            for dc in range(8):
                i = acq("A")
                pt, pts = pa(i)
                emit("tensor", lambda jj=jj, dc=dc, pt=pt: nc.tensor.transpose(
                     out=pt[:, 0:128], in_=xtok[jj][:, dc * 128:(dc + 1) * 128],
                     identity=idf[:, :]), reads=["xtok0", "idf"], writes=[pts],
                     tag="xT")
                emit("scalar", lambda dc=dc, j=j, pt=pt: nc.scalar.activation(
                     out=fm(xc, dc)[:, j * 128:(j + 1) * 128], in_=pt[:, 0:128],
                     func=AF.Copy), reads=[pts], writes=[f"xc{dc}"], tag="xTe")
                rel("A", i)

        def apply1(k, pR, pRs, pM, pMs):
            for dc in range(8):
                d = dc % 2
                emit("vector", lambda dc=dc, k=k, pR=pR, d=d: nc.vector.tensor_mul(
                     out=t1f[d][:, :], in0=fm(xc, dc)[:, k * 512:(k + 1) * 512],
                     in1=pR[:, :]), reads=[f"xc{dc}", pRs], writes=[f"t1f{d}"])
                emit("vector", lambda dc=dc, k=k, pM=pM, d=d: nc.vector.tensor_sub(
                     out=t1f[d][:, :], in0=t1f[d][:, :], in1=pM[:, :]),
                     reads=[f"t1f{d}", pMs], writes=[f"t1f{d}"])
                emit("vector", lambda dc=dc, k=k, d=d: nc.vector.tensor_scalar(
                     out=fm(xnm, dc)[:, k * 512:(k + 1) * 512], in0=t1f[d][:, :],
                     scalar1=nrmw[:, dc:dc + 1], scalar2=None, op0=ALU.mult),
                     reads=[f"t1f{d}", "nrmw"], writes=[f"xnm{dc}"])
        emit_ln(lambda dc: (fm(xc, dc), f"xc{dc}"), 8, WC, apply1, extras="fold")
        if c == DEBUG_CHUNK:
            dump("d_xc", xc[:, :], [f"xc{i}" for i in range(8)])
            dump("d_xnm", xnm[:, :], [f"xnm{i}" for i in range(8)])
        if MB & 1:
            pr.barrier()
        for oc in range(32):
            b = oc % 2
            dma("sync", d3(ipwb[b], 128), colslab(ipw_bf, oc * 128, (oc + 1) * 128),
                [f"ipwb{b}"], ["S.ip"], f"w_ip{b}", "ipld")
            i = acq("A")
            pt, pts = pa(i)
            for ks in range(8):
                emit("tensor", lambda b=b, ks=ks, pt=pt, s=(ks == 0),
                     e=(ks == 7): nc.tensor.matmul(out=pt[:, :],
                     lhsT=ipwb[b][:, ks * 128:(ks + 1) * 128],
                     rhs=fm(xnm, ks), start=s, stop=e),
                     reads=[f"ipwb{b}", f"xnm{ks}"], writes=[pts], tag="ipmm")
            if oc < 16:
                emit("scalar", lambda oc=oc, pt=pt: nc.scalar.activation(
                     out=upg(oc)[:, 3:3 + WC], in_=pt[:, :], func=AF.Copy),
                     reads=[pts], writes=[f"up{oc}"], tag="uev")
            else:
                emit("scalar", lambda oc=oc, pt=pt: nc.scalar.activation(
                     out=fm(srmo, oc - 16), in_=pt[:, :], func=AF.Silu),
                     reads=[pts], writes=[f"srmo{oc - 16}"], tag="srev")
            rel("A", i)
        if MB & 2:
            pr.barrier()
        for g in range(16):
            bb = g % 2
            emit("vector", lambda g=g, bb=bb: nc.vector.tensor_scalar(
                 out=facc[bb][:, :], in0=upg(g)[:, 0:WC],
                 scalar1=convw[:, g * 4:g * 4 + 1], scalar2=None, op0=ALU.mult),
                 reads=[f"up{g}", "convw"], writes=["facc"])
            for k in range(1, 4):
                emit("vector", lambda g=g, k=k, bb=bb: nc.vector.scalar_tensor_tensor(
                     out=facc[bb][:, :], in0=upg(g)[:, k:k + WC],
                     scalar=convw[:, g * 4 + k:g * 4 + k + 1], in1=facc[bb][:, :],
                     op0=ALU.mult, op1=ALU.add),
                     reads=[f"up{g}", "convw", "facc"], writes=["facc"])
            emit("scalar", lambda g=g, bb=bb: nc.scalar.activation(
                 out=fm(ucv, g), in_=facc[bb][:, :], func=AF.Silu,
                 bias=convb[:, g:g + 1]),
                 reads=["facc", "convb"], writes=[f"ucv{g}"], tag="cvsl")
            if c < NCH - 1:
                emit("vector", lambda g=g: nc.vector.tensor_copy(
                     out=upg(g)[:, 0:3], in_=upg(g)[:, WC:WC + 3]),
                     reads=[f"up{g}"], writes=[f"up{g}"], tag="halo")
        if c == DEBUG_CHUNK:
            dump("d_u", upad[:, :], [f"up{i}" for i in range(16)])
            dump("d_sr", srmo[:, :], [f"srmo{i}" for i in range(16)])
            dump("d_uc", ucv[:, :], [f"ucv{i}" for i in range(16)])
        if MB & 4:
            pr.barrier()
        idt, ib, ic = acq("A"), acq("A"), acq("A")
        pdt, pdts = pa(idt)
        pbm, pbms = pa(ib)
        pcm, pcms = pa(ic)
        for ks in range(16):
            emit("tensor", lambda ks=ks, p=pdt, s=(ks == 0), e=(ks == 15):
                 nc.tensor.matmul(out=p[0:64, :], lhsT=xpwsb[:, ks * 96:ks * 96 + 64],
                 rhs=fm(ucv, ks), start=s, stop=e),
                 reads=["xpwsb", f"ucv{ks}"], writes=[pdts], tag="xpdt")
            emit("tensor", lambda ks=ks, p=pbm, s=(ks == 0), e=(ks == 15):
                 nc.tensor.matmul(out=p[0:16, :], lhsT=xpwsb[:, ks * 96 + 64:ks * 96 + 80],
                 rhs=fm(ucv, ks), start=s, stop=e),
                 reads=["xpwsb", f"ucv{ks}"], writes=[pbms], tag="xpB")
            emit("tensor", lambda ks=ks, p=pcm, s=(ks == 0), e=(ks == 15):
                 nc.tensor.matmul(out=p[0:16, :], lhsT=xpwsb[:, ks * 96 + 80:ks * 96 + 96],
                 rhs=fm(ucv, ks), start=s, stop=e),
                 reads=["xpwsb", f"ucv{ks}"], writes=[pcms], tag="xpC")
        emit("scalar", lambda p=pdt: nc.scalar.activation(
             out=dtq[:, :], in_=p[0:64, :], func=AF.Copy), reads=[pdts],
             writes=["dtq"])
        emit("scalar", lambda p=pbm: nc.scalar.activation(
             out=brow[:, :], in_=p[0:16, :], func=AF.Copy), reads=[pbms],
             writes=["brow"])
        emit("scalar", lambda p=pcm: nc.scalar.activation(
             out=crow[:, :], in_=p[0:16, :], func=AF.Copy), reads=[pcms],
             writes=["crow"])
        rel("A", idt)
        rel("A", ib)
        rel("A", ic)
        for g in range(16):
            bb = g % 2
            i = acq("A")
            pt, pts = pa(i)
            emit("tensor", lambda g=g, pt=pt: nc.tensor.matmul(
                 out=pt[:, :], lhsT=dtwsb[:, g * 128:(g + 1) * 128], rhs=dtq[:, :],
                 start=True, stop=True), reads=["dtwsb", "dtq"], writes=[pts],
                 tag="dtmm")
            emit("scalar", lambda g=g, pt=pt, bb=bb: nc.scalar.activation(
                 out=facc[bb][:, :], in_=pt[:, :], func=AF.Exp,
                 bias=dtbc[:, g:g + 1]), reads=[pts, "dtbc"], writes=["facc"],
                 tag="dexp")
            rel("A", i)
            emit("scalar", lambda g=g, bb=bb: nc.scalar.activation(
                 out=fm(delta, g), in_=facc[bb][:, :], func=AF.Ln, bias=1.0),
                 reads=["facc"], writes=[f"dl{g}"], tag="dln")
            emit("vector", lambda g=g: nc.vector.tensor_mul(
                 out=fm(duv, g), in0=fm(delta, g), in1=fm(ucv, g)),
                 reads=[f"dl{g}", f"ucv{g}"], writes=[f"du{g}"])
        if c == DEBUG_CHUNK:
            dump("d_dtq", dtq[:, :], ["dtq"])
            dump("d_brow", brow[:, :], ["brow"])
            dump("d_crow", crow[:, :], ["crow"])
            dump("d_delta", delta[:, :], [f"dl{i}" for i in range(16)])
        if MB & 8:
            pr.barrier()
        for s_ in range(16):
            sbb = s_ % 2
            iB, iC = acq("A"), acq("A")
            pB, pBs = pa(iB)
            pC, pCs = pa(iC)
            emit("tensor", lambda s_=s_, p=pB: nc.tensor.matmul(
                 out=p[:, :], lhsT=pat[:, s_ * 128:(s_ + 1) * 128], rhs=brow[:, :],
                 start=True, stop=True), reads=["brow", "pat"], writes=[pBs],
                 tag="bexm")
            emit("tensor", lambda s_=s_, p=pC: nc.tensor.matmul(
                 out=p[:, :], lhsT=pat[:, s_ * 128:(s_ + 1) * 128], rhs=crow[:, :],
                 start=True, stop=True), reads=["crow", "pat"], writes=[pCs],
                 tag="cexm")
            emit("scalar", lambda p=pB: nc.scalar.activation(
                 out=bexs[:, :], in_=p[:, :], func=AF.Copy), reads=[pBs],
                 writes=["bex"], tag="bexe")
            emit("scalar", lambda p=pC: nc.scalar.activation(
                 out=cexs[:, :], in_=p[:, :], func=AF.Copy), reads=[pCs],
                 writes=["cex"], tag="cexe")
            rel("A", iB)
            rel("A", iC)
            for g in range(16):
                gb = g % 2
                col = g * 16 + s_
                emit("scalar", lambda g=g, s_=s_, gb=gb: nc.scalar.activation(
                     out=abuf[gb][:, :], in_=fm(delta, g), func=AF.Exp,
                     scale=acols[:, g * 16 + s_:g * 16 + s_ + 1]),
                     reads=[f"dl{g}", "acols"], writes=[f"ab{gb}"], tag="sexp")
                emit("vector", lambda g=g, gb=gb: nc.vector.tensor_mul(
                     out=bbuf[gb][:, :], in0=fm(duv, g), in1=bexs[:, :]),
                     reads=[f"du{g}", "bex"], writes=[f"bb{gb}"])
                emit("vector", lambda gb=gb, col=col: nc.vector.tensor_tensor_scan(
                     out=hbuf[gb][:, :], data0=abuf[gb][:, :], data1=bbuf[gb][:, :],
                     initial=hcar[:, col:col + 1], op0=ALU.mult, op1=ALU.add),
                     reads=[f"ab{gb}", f"bb{gb}", "hcar"], writes=[f"hb{gb}"],
                     tag="scan")
                if c < NCH - 1:
                    emit("vector", lambda gb=gb, col=col: nc.vector.tensor_copy(
                         out=hcar[:, col:col + 1], in_=hbuf[gb][:, WC - 1:WC]),
                         reads=[f"hb{gb}"], writes=["hcar"], tag="carry")
                emit("vector", lambda gb=gb: nc.vector.tensor_mul(
                     out=tcb[gb][:, :], in0=hbuf[gb][:, :], in1=cexs[:, :]),
                     reads=[f"hb{gb}", "cex"], writes=[f"tc{gb}"])
                if s_ == 0:
                    emit("vector", lambda g=g, gb=gb: nc.vector.tensor_copy(
                         out=yact(g), in_=tcb[gb][:, :]),
                         reads=[f"tc{gb}"], writes=[yslot(g)], tag="yin")
                else:
                    emit("vector", lambda g=g, gb=gb: nc.vector.tensor_add(
                         out=yact(g), in0=yact(g), in1=tcb[gb][:, :]),
                         reads=[f"tc{gb}", yslot(g)], writes=[yslot(g)], tag="yadd")
                if DEBUG and c == DEBUG_CHUNK and g < 2:
                    if "d_ys" not in dbg:
                        dbg["d_ys"] = nc.dram_tensor(
                            "d_ys", [128, 16 * 2 * WC], BF16,
                            kind="ExternalOutput").ap()
                    dma("gpsimd",
                        dbg["d_ys"][:, (s_ * 2 + g) * WC:(s_ * 2 + g + 1) * WC],
                        yact(g), [f"dbgys{s_}_{g}"], [yslot(g)],
                        f"dOs{g}", "dbg")
        if c == DEBUG_CHUNK and DEBUG:
            dma("gpsimd", dbg["d_y"][:, 0:8 * WC], yac1[:, :], ["dbgd_y"],
                [yslot(i) for i in range(8)], "dO1", "dbg")
            dma("gpsimd", dbg["d_y"][:, 8 * WC:16 * WC], yac2[:, :], ["dbgd_y"],
                [yslot(8 + i) for i in range(8)], "dO2", "dbg")
        if c == DEBUG_CHUNK and DEBUG:
            pass
        if MB & 16:
            pr.barrier()
        for g in range(16):
            emit("vector", lambda g=g: nc.vector.scalar_tensor_tensor(
                 out=yact(g), in0=fm(ucv, g), scalar=dcol[:, g:g + 1],
                 in1=yact(g), op0=ALU.mult, op1=ALU.add),
                 reads=[yslot(g), f"ucv{g}", "dcol"], writes=[yslot(g)])
            emit("vector", lambda g=g: nc.vector.tensor_mul(
                 out=yact(g), in0=yact(g), in1=fm(srmo, g)),
                 reads=[yslot(g), f"srmo{g}"], writes=[yslot(g)])
        if c == DEBUG_CHUNK and DEBUG:
            dma("gpsimd", dbg["d_yf"][:, 0:8 * WC], yac1[:, :], ["dbgd_yf"],
                [yslot(i) for i in range(8)], "dO3", "dbg")
            dma("gpsimd", dbg["d_yf"][:, 8 * WC:16 * WC], yac2[:, :], ["dbgd_yf"],
                [yslot(8 + i) for i in range(8)], "dO4", "dbg")
        if MB & 32:
            pr.barrier()
        emit("vector", lambda: nc.vector.memset(pr.fence_col[:, 3:4], 0.0),
             reads=[yslot(g) for g in range(16)], writes=["yfence"], tag="yfence")
        for oc in range(8):
            b = oc % 2
            dma("sync", d3(opwb[b], 128), colslab(opw_bf, oc * 128, (oc + 1) * 128),
                ["opwb0"], ["S.op"], "w_op0", "opld")
            if c == DEBUG_CHUNK and oc == 0 and DEBUG:
                dma("gpsimd", dbg["d_opw"][:, :], opwb[0][:, :], ["dbgow"],
                    ["ipwb0" if False else "opwb0"], "dO5", "dbg")
            i = acq("A")
            pt, pts = pa(i)
            for ks in range(16):
                emit("tensor", lambda b=b, ks=ks, pt=pt, s=(ks == 0), e=(ks == 15):
                     nc.tensor.matmul(out=pt[:, :],
                     lhsT=opwb[b][:, ks * 128:(ks + 1) * 128], rhs=yact(ks),
                     start=s, stop=e),
                     reads=["opwb0", yslot(ks), "yfence"], writes=[pts],
                     tag="opmm")
            emit("scalar", lambda oc=oc, pt=pt: nc.scalar.activation(
                 out=fm(srmo, oc), in_=pt[:, :], func=AF.Copy),
                 reads=[pts], writes=[f"srmo{oc}"], tag="moev")
            rel("A", i)

        def apply2(k, pR, pRs, pM, pMs):
            for dc in range(8):
                d = dc % 2
                emit("vector", lambda dc=dc, k=k, pR=pR, d=d: nc.vector.tensor_mul(
                     out=t1f[d][:, :], in0=fm(srmo, dc)[:, k * 512:(k + 1) * 512],
                     in1=pR[:, :]), reads=[f"srmo{dc}", pRs], writes=[f"t1f{d}"])
                emit("vector", lambda k=k, pM=pM, d=d: nc.vector.tensor_sub(
                     out=t1f[d][:, :], in0=t1f[d][:, :], in1=pM[:, :]),
                     reads=[f"t1f{d}", pMs], writes=[f"t1f{d}"])
                emit("vector", lambda dc=dc, k=k, d=d: nc.vector.tensor_add(
                     out=fm(xc, dc)[:, k * 512:(k + 1) * 512],
                     in0=fm(xc, dc)[:, k * 512:(k + 1) * 512], in1=t1f[d][:, :]),
                     reads=[f"t1f{d}", f"xc{dc}"], writes=[f"xc{dc}"])
        if c == DEBUG_CHUNK:
            dump("d_mo", srmo[:, 0:8 * WC], [f"srmo{i}" for i in range(8)])
        if MB & 64:
            pr.barrier()
        emit_ln(lambda dc: (fm(srmo, dc), f"srmo{dc}"), 8, WC, apply2, bf_in=True)
        if c == DEBUG_CHUNK:
            dump("d_x1", xc[:, :], [f"xc{i}" for i in range(8)])
        dma("gpsimd", xsp.rearrange("(g p) t -> p g t", p=128)[:, :, c * WC:(c + 1) * WC],
            xc[:, :].rearrange("p (g t) -> p g t", g=8),
            [f"sp{c}"], [f"xc{dc}" for dc in range(8)], "s_xc", "spill")

    pr.barrier()
    pm.close()
    stg.close()

    if DEV_STAGES >= 2:
        pf = ExitStack()

        def sbff(name, p, f, dt):
            return sb(name, p, f, dt, pf.enter_context)

        LNB["xht"] = [sbff(f"xhtF{i}", 128, 1024, BF16) for i in range(2)]
        LNB["sqt"] = [sbff(f"sqtF{i}", 128, 1024, BF16) for i in range(2)]
        xrot = [sbff(f"xrot{i}", 128, WF, F32) for i in range(2)]
        hful = sbff("hful", 128, 32 * WF, BF16)
        gful = sbff("gful", 128, 32 * WF, BF16)
        wsb = [sbff(f"wsb{i}", 128, 8 * 256, BF16) for i in range(2)]
        hwb = [sbff(f"hwb{i}", 128, 32 * 128, BF16) for i in range(2)]
        tmp1 = sbff("tmp1", 128, 2 * WF, BF16)

        def xnf(i, w=WF):
            return gful[:, i * w:(i + 1) * w]

        for hh in range(NHF):
            def get_xf(dc, hh=hh):
                d = dc % 2
                dma("sync", xrot[d][:, :],
                    xsp[dc * 128:(dc + 1) * 128, hh * WF:(hh + 1) * WF],
                    [f"xrot{d}"], [], f"x_r{d}", "xfld")
                return (xrot[d][:, :], f"xrot{d}")
            emit_ln(get_xf, 8, WF, lambda *a: None)
            ib4 = [acq("A") for _ in range(4)]
            pb4 = [pa(i) for i in ib4]
            for k in range(2):
                pR, pRs = pb4[2 * k]
                pM, pMs = pb4[2 * k + 1]
                emit("tensor", lambda pR=pR, k=k: nc.tensor.matmul(
                     out=pR[:, :], lhsT=ones_r[0:1, :],
                     rhs=rows[0:1, 2 * 1024 + k * 512:2 * 1024 + (k + 1) * 512],
                     start=True, stop=True), reads=["row2", "ones_r"], writes=[pRs])
                emit("tensor", lambda pM=pM, k=k: nc.tensor.matmul(
                     out=pM[:, :], lhsT=ones_r[0:1, :],
                     rhs=rows[0:1, 3 * 1024 + k * 512:3 * 1024 + (k + 1) * 512],
                     start=True, stop=True), reads=["row3", "ones_r"], writes=[pMs])
            for dc in range(8):
                d = dc % 2
                dma("sync", xrot[d][:, :],
                    xsp[dc * 128:(dc + 1) * 128, hh * WF:(hh + 1) * WF],
                    [f"xrot{d}"], [], f"x_r{d}", "xfld2")
                for k in range(2):
                    pR, pRs = pb4[2 * k]
                    pM, pMs = pb4[2 * k + 1]
                    emit("vector", lambda dc=dc, k=k, pR=pR, d=d: nc.vector.tensor_mul(
                         out=t1f[d][:, :], in0=xrot[d][:, k * 512:(k + 1) * 512],
                         in1=pR[:, :]), reads=[f"xrot{d}", pRs], writes=[f"t1f{d}"])
                    emit("vector", lambda dc=dc, k=k, pM=pM, d=d: nc.vector.tensor_sub(
                         out=xnf(dc)[:, k * 512:(k + 1) * 512], in0=t1f[d][:, :],
                         in1=pM[:, :]), reads=[f"t1f{d}", pMs], writes=[f"gf{dc}"])
            for i in ib4:
                rel("A", i)
            if hh == 0 and DEBUG:
                for nm in ("f_xnf", "f_h", "f_g1", "f_g2", "f_fo"):
                    if nm not in dbg:
                        dbg[nm] = nc.dram_tensor(nm, [128, 32 * WF], BF16,
                                                 kind="ExternalOutput").ap()
                dma("gpsimd", dbg["f_xnf"][:, 0:8 * WF], gful[:, 0:8 * WF],
                    ["dbgf1"], [f"gf{i}" for i in range(8)], "dO6", "dbg")
            for ocb in range(16):
                b = ocb % 2
                dma("sync", d3(wsb[b], 256), colslab(w1T, ocb * 256, (ocb + 1) * 256),
                    [f"wsb{b}"], ["S.w1T"], f"w_f{b}", "w1ld")
                for oci in range(2):
                    oc = ocb * 2 + oci
                    iB = acq("B")
                    ptB, ptBs = pb(iB)
                    for ks in range(8):
                        for k in range(2):
                            emit("tensor", lambda b=b, oci=oci, ks=ks, k=k, pt=ptB,
                                 s=(ks == 0), e=(ks == 7): nc.tensor.matmul(
                                 out=pt[:, k * 512:(k + 1) * 512],
                                 lhsT=wsb[b][:, ks * 256 + oci * 128:ks * 256 + (oci + 1) * 128],
                                 rhs=xnf(ks)[:, k * 512:(k + 1) * 512],
                                 start=s, stop=e),
                                 reads=[f"wsb{b}", f"gf{ks}"], writes=[ptBs],
                                 tag="w1mm")
                    emit("scalar", lambda oc=oc, oci=oci, pt=ptB: nc.scalar.activation(
                         out=tmp1[:, oci * WF:(oci + 1) * WF], in_=pt[:, :],
                         func=AF.Silu, scale=s1c[:, oc:oc + 1]),
                         reads=[ptBs, "s1c"], writes=[f"tmp1{oci}"], tag="w1ev")
                    rel("B", iB)
                b2 = (ocb + 1) % 2
                dma("sync", d3(wsb[b2], 256), colslab(w2T, ocb * 256, (ocb + 1) * 256),
                    [f"wsb{b2}"], ["S.w2T"], f"w_f{b2}", "w2ld")
                for oci in range(2):
                    oc = ocb * 2 + oci
                    iB = acq("B")
                    ptB, ptBs = pb(iB)
                    for ks in range(8):
                        for k in range(2):
                            emit("tensor", lambda b2=b2, oci=oci, ks=ks, k=k, pt=ptB,
                                 s=(ks == 0), e=(ks == 7): nc.tensor.matmul(
                                 out=pt[:, k * 512:(k + 1) * 512],
                                 lhsT=wsb[b2][:, ks * 256 + oci * 128:ks * 256 + (oci + 1) * 128],
                                 rhs=xnf(ks)[:, k * 512:(k + 1) * 512],
                                 start=s, stop=e),
                                 reads=[f"wsb{b2}", f"gf{ks}"], writes=[ptBs],
                                 tag="w2mm")
                    for k in range(2):
                        emit("vector", lambda oc=oc, oci=oci, k=k, pt=ptB:
                             nc.vector.scalar_tensor_tensor(
                             out=fm(hful, oc, WF)[:, k * 512:(k + 1) * 512],
                             in0=pt[:, k * 512:(k + 1) * 512],
                             scalar=s2c[:, oc:oc + 1],
                             in1=tmp1[:, oci * WF + k * 512:oci * WF + (k + 1) * 512],
                             op0=ALU.mult, op1=ALU.mult),
                             reads=[ptBs, "s2c", f"tmp1{oci}"], writes=[f"hf{oc}"],
                             tag="hmul")
                    rel("B", iB)
            if hh == 0 and DEBUG:
                dma("gpsimd", dbg["f_h"][:, :], hful[:, :], ["dbgf2"],
                    [f"hf{i}" for i in range(32)], "dO7", "dbg")
            for (wt, wslot, src, srcs, dst, dsts, scol, scs) in (
                    (h1T, "S.h1T", hful, "hf", gful, "gf", h1sc, "h1sc"),
                    (h2T, "S.h2T", gful, "gf", hful, "hf", h2sc, "h2sc")):
                for oc2 in range(32):
                    b = oc2 % 2
                    dma("sync", d3(hwb[b], 128), colslab(wt, oc2 * 128, (oc2 + 1) * 128),
                        [f"hwb{b}"], [wslot], f"w_h{b}", "hld")
                    iB = acq("B")
                    ptB, ptBs = pb(iB)
                    for ks in range(32):
                        for k in range(2):
                            emit("tensor", lambda b=b, ks=ks, k=k, pt=ptB, src=src,
                                 s=(ks == 0), e=(ks == 31): nc.tensor.matmul(
                                 out=pt[:, k * 512:(k + 1) * 512],
                                 lhsT=hwb[b][:, ks * 128:(ks + 1) * 128],
                                 rhs=fm(src, ks, WF)[:, k * 512:(k + 1) * 512],
                                 start=s, stop=e),
                                 reads=[f"hwb{b}", f"{srcs}{ks}"], writes=[ptBs],
                                 tag="hmm")
                    emit("scalar", lambda oc2=oc2, pt=ptB, scol=scol, dst=dst:
                         nc.scalar.activation(out=fm(dst, oc2, WF), in_=pt[:, :],
                         func=AF.Gelu, scale=scol[:, oc2:oc2 + 1]),
                         reads=[ptBs, scs], writes=[f"{dsts}{oc2}"], tag="gev")
                    rel("B", iB)
            if hh == 0 and DEBUG:
                dma("gpsimd", dbg["f_g1"][:, :], gful[:, :], ["dbgf3"],
                    [f"gf{i}" for i in range(32)], "dO8", "dbg")
                dma("gpsimd", dbg["f_g2"][:, :], hful[:, :], ["dbgf4"],
                    [f"hf{i}" for i in range(32)], "dO9", "dbg")
            moslot = []
            for oc3 in range(8):
                b = oc3 % 2
                dma("sync", d3(hwb[b], 128), colslab(w3T, oc3 * 128, (oc3 + 1) * 128),
                    [f"hwb{b}"], ["S.w3T"], f"w_h{b}", "w3ld")
                iB = acq("B")
                ptB, ptBs = pb(iB)
                for ks in range(32):
                    for k in range(2):
                        emit("tensor", lambda b=b, ks=ks, k=k, pt=ptB, s=(ks == 0),
                             e=(ks == 31): nc.tensor.matmul(
                             out=pt[:, k * 512:(k + 1) * 512],
                             lhsT=hwb[b][:, ks * 128:(ks + 1) * 128],
                             rhs=fm(hful, ks, WF)[:, k * 512:(k + 1) * 512],
                             start=s, stop=e),
                             reads=[f"hwb{b}", f"hf{ks}"], writes=[ptBs], tag="w3mm")
                emit("scalar", lambda oc3=oc3, pt=ptB: nc.scalar.activation(
                     out=fm(gful, 16 + oc3, WF), in_=pt[:, :], func=AF.Identity,
                     scale=s3c[:, oc3:oc3 + 1]),
                     reads=[ptBs, "s3c"], writes=[f"gf{16 + oc3}"], tag="foev")
                rel("B", iB)
                moslot.append(f"gf{16 + oc3}")

            if hh == 0 and DEBUG:
                dma("gpsimd", dbg["f_fo"][:, 0:8 * WF], gful[:, 16 * WF:24 * WF],
                    ["dbgf5"], [f"gf{16 + i}" for i in range(8)], "dO10", "dbg")
            emit_ln(lambda dc: (fm(gful, 16 + dc, WF), f"gf{16 + dc}"), 8, WF,
                    lambda *a: None, bf_in=True)
            ib4 = [acq("A") for _ in range(4)]
            pb4 = [pa(i) for i in ib4]
            for k in range(2):
                pR, pRs = pb4[2 * k]
                pM, pMs = pb4[2 * k + 1]
                emit("tensor", lambda pR=pR, k=k: nc.tensor.matmul(
                     out=pR[:, :], lhsT=ones_r[0:1, :],
                     rhs=rows[0:1, 2 * 1024 + k * 512:2 * 1024 + (k + 1) * 512],
                     start=True, stop=True), reads=["row2", "ones_r"], writes=[pRs])
                emit("tensor", lambda pM=pM, k=k: nc.tensor.matmul(
                     out=pM[:, :], lhsT=ones_r[0:1, :],
                     rhs=rows[0:1, 3 * 1024 + k * 512:3 * 1024 + (k + 1) * 512],
                     start=True, stop=True), reads=["row3", "ones_r"], writes=[pMs])
            for dc in range(8):
                d = dc % 2
                dma("sync", xrot[d][:, :],
                    xsp[dc * 128:(dc + 1) * 128, hh * WF:(hh + 1) * WF],
                    [f"xrot{d}"], [], f"x_r{d}", "xfld3")
                for k in range(2):
                    pR, pRs = pb4[2 * k]
                    pM, pMs = pb4[2 * k + 1]
                    emit("vector", lambda dc=dc, k=k, pR=pR, d=d: nc.vector.tensor_mul(
                         out=t1f[d][:, :],
                         in0=fm(gful, 16 + dc, WF)[:, k * 512:(k + 1) * 512],
                         in1=pR[:, :]), reads=[f"gf{16 + dc}", pRs],
                         writes=[f"t1f{d}"])
                    emit("vector", lambda k=k, pM=pM, d=d: nc.vector.tensor_sub(
                         out=t1f[d][:, :], in0=t1f[d][:, :], in1=pM[:, :]),
                         reads=[f"t1f{d}", pMs], writes=[f"t1f{d}"])
                    emit("vector", lambda dc=dc, k=k, d=d: nc.vector.tensor_add(
                         out=xrot[d][:, k * 512:(k + 1) * 512],
                         in0=xrot[d][:, k * 512:(k + 1) * 512], in1=t1f[d][:, :]),
                         reads=[f"t1f{d}", f"xrot{d}"], writes=[f"xrot{d}"])
                dma("gpsimd",
                    xsp[dc * 128:(dc + 1) * 128, hh * WF:(hh + 1) * WF],
                    xrot[d][:, :], [f"spf{hh}"], [f"xrot{d}"], f"s_f{d}", "spillF")
            for i in ib4:
                rel("A", i)
        pr.barrier()
        pf.close()

    if DEV_STAGES >= 3:
        pa_ = ExitStack()

        def sba(name, p, f, dt):
            return sb(name, p, f, dt, pa_.enter_context)

        LNB["xht"] = [sba(f"xhtA{i}", 128, 512, BF16) for i in range(2)]
        LNB["sqt"] = [sba(f"sqtA{i}", 128, 512, BF16) for i in range(2)]
        xna = sba("xna", 128, 8 * T, BF16)
        qf = sba("qf", 128, 8 * T, BF16)
        kf = sba("kf", 128, 8 * T, BF16)
        vsb = sba("vsb", 128, 16 * 1040, BF16)
        wvsb = sba("wvsb", 128, 8 * 1024, BF16)
        sct = [sba(f"sct{i}", 128, T, BF16) for i in range(2)]
        yraw = sba("yraw", 64, T, F32)
        awb = [sba(f"awb{i}", 128, 8 * 128, BF16) for i in range(2)]
        x2c = [sba(f"x2c{i}", 128, WC, F32) for i in range(2)]

        for tt in range(16):
            emit("vector", lambda tt=tt: nc.vector.memset(
                 vsb[:, tt * 1040:(tt + 1) * 1040].rearrange(
                     "p (h o) -> p h o", o=65)[:, :, 64:65], 1.0),
                 writes=[f"vsb{tt}"], tag="vones")

        for ca in range(NCH):
            for dc in range(8):
                dma("sync", x2c[dc % 2][:, :],
                    xsp[dc * 128:(dc + 1) * 128, ca * WC:(ca + 1) * WC],
                    [f"x2c{dc % 2}"], [], f"x_c{dc % 2}", "x2ld")
                emit("vector", lambda dc=dc, ca=ca: nc.vector.tensor_copy(
                     out=fm(xna, dc, T)[:, ca * WC:(ca + 1) * WC],
                     in_=x2c[dc % 2][:, :]),
                     reads=[f"x2c{dc % 2}"], writes=[f"xna{dc}"], tag="xcast")

            def applyA(k, pR, pRs, pM, pMs, ca=ca):
                for dc in range(8):
                    d = dc % 2
                    emit("vector", lambda dc=dc, k=k, pR=pR, d=d, ca=ca:
                         nc.vector.tensor_mul(out=t1f[d][:, :],
                         in0=fm(xna, dc, T)[:, ca * WC + k * 512:ca * WC + (k + 1) * 512],
                         in1=pR[:, :]), reads=[f"xna{dc}", pRs], writes=[f"t1f{d}"])
                    emit("vector", lambda k=k, pM=pM, d=d: nc.vector.tensor_sub(
                         out=t1f[d][:, :], in0=t1f[d][:, :], in1=pM[:, :]),
                         reads=[f"t1f{d}", pMs], writes=[f"t1f{d}"])
                    emit("vector", lambda dc=dc, k=k, d=d, ca=ca: nc.vector.tensor_scalar(
                         out=fm(xna, dc, T)[:, ca * WC + k * 512:ca * WC + (k + 1) * 512],
                         in0=t1f[d][:, :], scalar1=lngc[:, dc:dc + 1],
                         scalar2=lnbc[:, dc:dc + 1], op0=ALU.mult, op1=ALU.add),
                         reads=[f"t1f{d}", "lngc", "lnbc"], writes=[f"xna{dc}"])
            emit_ln(lambda dc, ca=ca: (fm(xna, dc, T)[:, ca * WC:(ca + 1) * WC],
                    f"xna{dc}"), 8, WC, applyA, extras="fold", bf_in=True)

        if DEBUG:
            for nm, shp, dt_ in (("a_xna", [128, 8 * T], BF16),
                                 ("a_qf", [128, 8 * T], BF16),
                                 ("a_kf", [128, 8 * T], BF16),
                                 ("a_vsb", [128, 16 * 1040], BF16),
                                 ("a_y", [128, 8 * T], BF16)):
                dbg[nm] = nc.dram_tensor(nm, shp, dt_, kind="ExternalOutput").ap()
            dma("gpsimd", dbg["a_xna"][:, :], xna[:, :], ["dbgaxna"],
                [f"xna{i}" for i in range(8)], "dO11", "dbg")
        for (wsrc, wslot, dst, dslot, bcol, bslot) in (
                (wq_bf, "S.wq", qf, "qf", bqc, "bqc"),
                (wk_bf, "S.wk", kf, "kf", bkc, "bkc")):
            for oc in range(8):
                b = oc % 2
                dma("sync", d3(awb[b], 128), colslab(wsrc, oc * 128, (oc + 1) * 128),
                    [f"awb{b}"], [wslot], f"w_a{b}", "qkld")
                for k4 in range(2):
                    iB = acq("B")
                    ptB, ptBs = pb(iB)
                    for ks in range(8):
                        for k in range(2):
                            emit("tensor", lambda b=b, ks=ks, k=k, k4=k4, pt=ptB,
                                 s=(ks == 0), e=(ks == 7): nc.tensor.matmul(
                                 out=pt[:, k * 512:(k + 1) * 512],
                                 lhsT=awb[b][:, ks * 128:(ks + 1) * 128],
                                 rhs=fm(xna, ks, T)[:, k4 * 1024 + k * 512:k4 * 1024 + (k + 1) * 512],
                                 start=s, stop=e),
                                 reads=[f"awb{b}", f"xna{ks}"], writes=[ptBs],
                                 tag="qkmm")
                    emit("scalar", lambda oc=oc, k4=k4, pt=ptB, dst=dst, bcol=bcol,
                         bslot=bslot: nc.scalar.activation(
                         out=fm(dst, oc, T)[:, k4 * 1024:(k4 + 1) * 1024],
                         in_=pt[:, :], func=AF.Identity, bias=bcol[:, oc:oc + 1]),
                         reads=[ptBs, bslot], writes=[f"{dslot}{oc}"], tag="qkev")
                    rel("B", iB)

        for ks in range(8):
            dma("sync", wvsb[:, ks * 1024:(ks + 1) * 1024],
                wv_bf[ks * 128:(ks + 1) * 128, :], [f"wvsb{ks}"], ["S.wv"],
                "dPv", "wvld")
        for tt in range(16):
            iB = acq("B")
            ptB, ptBs = pb(iB)
            for ks in range(8):
                for k in range(2):
                    emit("tensor", lambda tt=tt, ks=ks, k=k, pt=ptB, s=(ks == 0):
                         nc.tensor.matmul(out=pt[:, k * 512:(k + 1) * 512],
                         lhsT=fm(xna, ks, T)[:, tt * 128:(tt + 1) * 128],
                         rhs=wvsb[:, ks * 1024 + k * 512:ks * 1024 + (k + 1) * 512],
                         start=s, stop=False),
                         reads=[f"xna{ks}", f"wvsb{ks}"], writes=[ptBs], tag="vmm")
            for k in range(2):
                emit("tensor", lambda k=k, pt=ptB: nc.tensor.matmul(
                     out=pt[:, k * 512:(k + 1) * 512], lhsT=ones_rb[0:1, :],
                     rhs=bvr[0:1, k * 512:(k + 1) * 512], start=False, stop=True),
                     reads=["ones_rb", "bvr"], writes=[ptBs], tag="vbias")
            emit("scalar", lambda tt=tt, pt=ptB: nc.scalar.activation(
                 out=vsb[:, tt * 1040:(tt + 1) * 1040].rearrange(
                     "p (h o) -> p h o", o=65)[:, :, 0:64],
                 in_=pt[:, :].rearrange("p (h o) -> p h o", o=64), func=AF.Copy),
                 reads=[ptBs], writes=[f"vsb{tt}"], tag="vev")
            rel("B", iB)

        if DEBUG:
            dma("gpsimd", dbg["a_qf"][:, :], qf[:, :], ["dbgaqf"],
                [f"qf{i}" for i in range(8)], "dO12", "dbg")
            dma("gpsimd", dbg["a_kf"][:, :], kf[:, :], ["dbgakf"],
                [f"kf{i}" for i in range(8)], "dO13", "dbg")
            dma("gpsimd", dbg["a_vsb"][:, :], vsb[:, :], ["dbgavsb"],
                [f"vsb{i}" for i in range(16)], "dO14", "dbg")
        for h in range(NH):
            oc, po = h // 2, (h % 2) * 64
            iy = [acq("A") for _ in range(4)]
            pys = [pa(i) for i in iy]
            for j in range(16):
                jb = j % 2
                iB = acq("B")
                ptB, ptBs = pb(iB)
                for u in range(2):
                    for k in range(2):
                        emit("tensor", lambda oc=oc, po=po, j=j, u=u, k=k, pt=ptB:
                             nc.tensor.matmul(
                             out=pt[:, k * 512:(k + 1) * 512],
                             lhsT=fm(kf, oc, T)[po:po + 64, j * 128:(j + 1) * 128],
                             rhs=fm(qf, oc, T)[po:po + 64, u * 1024 + k * 512:u * 1024 + (k + 1) * 512],
                             start=True, stop=True),
                             reads=[f"kf{oc}", f"qf{oc}"], writes=[ptBs], tag="qkT")
                    emit("scalar", lambda u=u, jb=jb, pt=ptB: nc.scalar.activation(
                         out=sct[jb][:, u * 1024:(u + 1) * 1024], in_=pt[:, :],
                         func=AF.Exp, scale=0.125),
                         reads=[ptBs], writes=[f"sct{jb}"], tag="sexpA")
                rel("B", iB)
                for t4 in range(4):
                    py, pys_ = pys[t4]
                    emit("tensor", lambda h=h, j=j, t4=t4, jb=jb, py=py,
                         s=(j == 0), e=(j == 15): nc.tensor.matmul(
                         out=py[0:65, :],
                         lhsT=vsb[:, j * 1040 + h * 65:j * 1040 + (h + 1) * 65],
                         rhs=sct[jb][:, t4 * 512:(t4 + 1) * 512],
                         start=s, stop=e),
                         reads=[f"vsb{j}", f"sct{jb}"], writes=[pys_], tag="avmm")
            for t4 in range(4):
                py, pys_ = pys[t4]
                emit("scalar", lambda t4=t4, py=py: nc.scalar.activation(
                     out=rows[0:1, t4 * 512:(t4 + 1) * 512],
                     in_=py[64:65, :], func=AF.Ln),
                     reads=[pys_], writes=["row0", "row1"], tag="zln")
            emit("scalar", lambda: nc.scalar.activation(
                 out=rows[0:1, 0:2048], in_=rows[0:1, 0:2048],
                 func=AF.Exp, scale=-1.0),
                 reads=["row0", "row1"], writes=["row0", "row1"], tag="zexp")
            for t4 in range(4):
                py, pys_ = pys[t4]
                emit("vector", lambda t4=t4, py=py: nc.vector.tensor_copy(
                     out=yraw[:, t4 * 512:(t4 + 1) * 512], in_=py[0:64, :]),
                     reads=[pys_], writes=["yraw"], tag="yev")
            for i in iy:
                rel("A", i)
            for t4 in range(4):
                i = acq("A")
                py, pys_ = pa(i)
                emit("tensor", lambda t4=t4, py=py: nc.tensor.matmul(
                     out=py[0:64, :], lhsT=ones_r[0:1, 0:64],
                     rhs=rows[0:1, t4 * 512:(t4 + 1) * 512],
                     start=True, stop=True),
                     reads=["row0", "row1", "ones_r"], writes=[pys_], tag="rzb")
                emit("vector", lambda h=h, oc=oc, po=po, t4=t4, py=py:
                     nc.vector.tensor_mul(
                     out=fm(qf, oc, T)[po:po + 64, t4 * 512:(t4 + 1) * 512],
                     in0=yraw[:, t4 * 512:(t4 + 1) * 512], in1=py[0:64, :]),
                     reads=["yraw", pys_], writes=[f"qf{oc}"],
                     tag="ynrm")
                rel("A", i)
        if DEBUG:
            dma("gpsimd", dbg["a_y"][:, :], qf[:, :], ["dbgay"],
                [f"qf{i}" for i in range(8)], "dO15", "dbg")
        for ca in range(NCH):
            rbA = emit_ln(lambda dc, ca=ca: (fm(qf, dc, T)[:, ca * WC:(ca + 1) * WC],
                          f"qf{dc}"), 8, WC, lambda *a: None, bf_in=True)
            RRA, REA = rbA + 2, rbA + 3
            for dc in range(8):
                d = dc % 2
                dma("sync", x2c[d][:, :],
                    xsp[dc * 128:(dc + 1) * 128, ca * WC:(ca + 1) * WC],
                    [f"x2c{d}"], [], f"x_c{d}", "x2ld2")
                iR, iM = acq("A"), acq("A")
                pR, pRs = pa(iR)
                pM, pMs = pa(iM)
                emit("tensor", lambda pR=pR: nc.tensor.matmul(
                     out=pR[:, :], lhsT=ones_r[0:1, :],
                     rhs=rows[0:1, RRA * 1024:RRA * 1024 + 512],
                     start=True, stop=True),
                     reads=[f"row{RRA}", "ones_r"], writes=[pRs])
                emit("tensor", lambda pM=pM: nc.tensor.matmul(
                     out=pM[:, :], lhsT=ones_r[0:1, :],
                     rhs=rows[0:1, REA * 1024:REA * 1024 + 512],
                     start=True, stop=True),
                     reads=[f"row{REA}", "ones_r"], writes=[pMs])
                emit("vector", lambda dc=dc, d=d, pR=pR, ca=ca: nc.vector.tensor_mul(
                     out=t1f[d][:, :],
                     in0=fm(qf, dc, T)[:, ca * WC:(ca + 1) * WC], in1=pR[:, :]),
                     reads=[f"qf{dc}", pRs], writes=[f"t1f{d}"])
                emit("vector", lambda d=d, pM=pM: nc.vector.tensor_sub(
                     out=t1f[d][:, :], in0=t1f[d][:, :], in1=pM[:, :]),
                     reads=[f"t1f{d}", pMs], writes=[f"t1f{d}"])
                emit("vector", lambda d=d: nc.vector.tensor_add(
                     out=x2c[d][:, :], in0=x2c[d][:, :], in1=t1f[d][:, :]),
                     reads=[f"t1f{d}", f"x2c{d}"], writes=[f"x2c{d}"])
                rel("A", iR)
                rel("A", iM)
                dma("gpsimd", xsp[dc * 128:(dc + 1) * 128, ca * WC:(ca + 1) * WC],
                    x2c[d][:, :], [f"spA{ca}"], [f"x2c{d}"], f"s_a{d}", "spillA")
        pr.barrier()
        pa_.close()

    if DEV_STAGES >= 4:
        pl = ExitStack()

        def sbl(name, p, f, dt):
            return sb(name, p, f, dt, pl.enter_context)

        LNB["xht"] = [sbl(f"xhtL{i}", 128, 512, BF16) for i in range(2)]
        LNB["sqt"] = [sbl(f"sqtL{i}", 128, 512, BF16) for i in range(2)]
        xnq = sbl("xnq", 128, 8 * WC, F32)
        xq = sbl("xq", 128, 8 * WC, BF16)
        xq8 = sbl("xq8", 128, 512, I8)
        h1d = [sbl(f"h1d{i}", 128, 32 * WC, BF16) for i in range(2)]
        hch = sbl("hch", 128, 32 * WC, BF16)
        fo = xq
        qwb = [sbl(f"qwb{i}", 128, 8 * 256, BF16) for i in range(2)]
        q2b = [sbl(f"q2b{i}", 128, 32 * 128, BF16) for i in range(2)]
        cmx = sbl("cmx", 128, 512, F32)
        qab = [sbl(f"qab{i}", 128, 512, F32) for i in range(2)]
        qmc = sbl("qmc", 128, 8, F32)
        onesq = sbl("onesq", 128, 128, F32)
        qsb = sbl("qsb", 128, 512, F32)
        sbc = sbl("sbc", 128, 512, F32)
        isbc = sbl("isbc", 128, 512, F32)
        hp = sbl("hp", 128, WC + 2, BF16)
        hsv = sbl("hsv", 128, 4 * 32, BF16)

        emit("vector", lambda: nc.vector.memset(onesq[:, :], 1.0), writes=["onesq"])

        def quant_act(src_fn, src_slots, ndc, qi_col, dst_fn, dst_slots, wq_col):
            for dc in range(ndc):
                d = dc % 2
                emit("scalar", lambda dc=dc, d=d: nc.scalar.activation(
                     out=qab[d][:, :], in_=src_fn(dc), func=AF.Abs),
                     reads=[src_slots(dc)], writes=[f"qab{d}"], tag="qabs")
                if dc == 0:
                    emit("vector", lambda d=d: nc.vector.tensor_copy(
                         out=cmx[:, :], in_=qab[d][:, :]),
                         reads=[f"qab{d}"], writes=["cmx"])
                else:
                    emit("vector", lambda d=d: nc.vector.tensor_max(
                         out=cmx[:, :], in0=cmx[:, :], in1=qab[d][:, :]),
                         reads=["cmx", f"qab{d}"], writes=["cmx"])
            iA = acq("A")
            pt, pts = pa(iA)
            for k in range(4):
                emit("tensor", lambda k=k, pt=pt: nc.tensor.transpose(
                     out=pt[:, k * 128:(k + 1) * 128], in_=cmx[:, k * 128:(k + 1) * 128],
                     identity=idf[:, :]), reads=["cmx", "idf"], writes=[pts],
                     tag="qT")
            for k in range(4):
                emit("vector", lambda k=k, pt=pt: nc.vector.tensor_reduce(
                     out=qmc[:, k:k + 1], in_=pt[:, k * 128:(k + 1) * 128],
                     axis=AX.X, op=ALU.max), reads=[pts], writes=["qmc"],
                     tag="qred")
            rel("A", iA)
            emit("vector", lambda: nc.vector.tensor_scalar(
                 out=qmc[:, 0:4], in0=qmc[:, 0:4], scalar1=1e-5, scalar2=1.0 / 127.0,
                 op0=ALU.max, op1=ALU.mult), reads=["qmc"], writes=["qmc"])
            emit("vector", lambda: nc.vector.reciprocal(
                 out=qmc[:, 4:8], in_=qmc[:, 0:4]), reads=["qmc"], writes=["qmc"])
            for (cbase, dsttile, dslot, wscale) in ((4, sbc, "sbc", None),
                                                    (0, isbc, "isbc", wq_col)):
                for k in range(4):
                    emit("vector", lambda k=k, cbase=cbase: nc.vector.tensor_scalar(
                         out=qsb[:, k * 128:(k + 1) * 128], in0=onesq[:, :],
                         scalar1=qmc[:, cbase + k:cbase + k + 1], scalar2=None,
                         op0=ALU.mult), reads=["qmc", "onesq"], writes=["qsb"])
                iA = acq("A")
                pt, pts = pa(iA)
                for k in range(4):
                    emit("tensor", lambda k=k, pt=pt: nc.tensor.transpose(
                         out=pt[:, k * 128:(k + 1) * 128],
                         in_=qsb[:, k * 128:(k + 1) * 128], identity=idf[:, :]),
                         reads=["qsb", "idf"], writes=[pts], tag="qbT")
                if wscale is None:
                    emit("scalar", lambda pt=pt, dsttile=dsttile: nc.scalar.activation(
                         out=dsttile[:, :], in_=pt[:, :], func=AF.Copy),
                         reads=[pts], writes=[dslot], tag="qbe")
                else:
                    emit("scalar", lambda pt=pt, dsttile=dsttile, wscale=wscale:
                         nc.scalar.activation(out=dsttile[:, :], in_=pt[:, :],
                         func=AF.Identity, scale=qsc[:, wscale:wscale + 1]),
                         reads=[pts, "qsc"], writes=[dslot], tag="qbe")
                rel("A", iA)
            for dc in range(ndc):
                d = dc % 2
                emit("vector", lambda dc=dc, d=d: nc.vector.tensor_mul(
                     out=t1f[d][:, :], in0=src_fn(dc), in1=sbc[:, :]),
                     reads=[src_slots(dc), "sbc"], writes=[f"t1f{d}"])
                emit("vector", lambda d=d: nc.vector.tensor_copy(
                     out=xq8[:, :], in_=t1f[d][:, :]),
                     reads=[f"t1f{d}"], writes=["xq8"])
                emit("scalar", lambda dc=dc: nc.scalar.activation(
                     out=dst_fn(dc), in_=xq8[:, :], func=AF.Copy),
                     reads=["xq8"], writes=[dst_slots(dc)], tag="qi8")

        def l1_tail(c, cur, prv):
            for oci in range(32):
                emit("vector", lambda oci=oci, prv=prv: nc.vector.tensor_copy(
                     out=hp[:, 1:1 + WC], in_=fm(h1d[prv], oci)),
                     reads=[f"hd{prv}_{oci}"], writes=["hp"])
                if c > 0:
                    emit("vector", lambda oci=oci, c=c: nc.vector.tensor_copy(
                         out=hp[:, 0:1], in_=hsv[:, (c - 1) * 32 + oci:(c - 1) * 32 + oci + 1]),
                         reads=[f"hsv{c - 1}"], writes=["hp"])
                else:
                    emit("vector", lambda: nc.vector.memset(hp[:, 0:1], 0.0),
                         writes=["hp"])
                if c < NCH - 1:
                    emit("vector", lambda oci=oci, cur=cur: nc.vector.tensor_copy(
                         out=hp[:, 1 + WC:2 + WC], in_=fm(h1d[cur], oci)[:, 0:1]),
                         reads=[f"hd{cur}_{oci}"], writes=["hp"])
                else:
                    emit("vector", lambda: nc.vector.memset(hp[:, 1 + WC:2 + WC], 0.0),
                         writes=["hp"])
                d = oci % 2
                emit("vector", lambda oci=oci, d=d: nc.vector.tensor_scalar(
                     out=facc2[d][:, :], in0=hp[:, 0:WC],
                     scalar1=cvw1[:, oci * 3:oci * 3 + 1], scalar2=None, op0=ALU.mult),
                     reads=["hp", "cvw1"], writes=[f"fc2{d}"])
                for k in range(1, 3):
                    emit("vector", lambda oci=oci, k=k, d=d: nc.vector.scalar_tensor_tensor(
                         out=facc2[d][:, :], in0=hp[:, k:k + WC],
                         scalar=cvw1[:, oci * 3 + k:oci * 3 + k + 1],
                         in1=facc2[d][:, :], op0=ALU.mult, op1=ALU.add),
                         reads=["hp", "cvw1", f"fc2{d}"], writes=[f"fc2{d}"])
                emit("scalar", lambda oci=oci, d=d: nc.scalar.activation(
                     out=fm(hch, oci), in_=facc2[d][:, :], func=AF.Silu,
                     bias=cvb1[:, oci:oci + 1]),
                     reads=[f"fc2{d}", "cvb1"], writes=[f"hch{oci}"], tag="cv2sl")
            quant_act(lambda dc: fm(hch, dc), lambda dc: f"hch{dc}", 32, 0,
                      lambda dc: fm(hch, dc), lambda dc: f"hch{dc}", 1)
            for oc in range(8):
                b = oc % 2
                dma("sync", d3(q2b[b], 128), colslab(q2T, oc * 128, (oc + 1) * 128),
                    [f"q2b{b}"], ["S.q2T"], f"w_2{b}", "q2ld")
                iA = acq("A")
                pt, pts = pa(iA)
                for ks in range(32):
                    emit("tensor", lambda b=b, ks=ks, pt=pt, s=(ks == 0), e=(ks == 31):
                         nc.tensor.matmul(out=pt[:, :],
                         lhsT=q2b[b][:, ks * 128:(ks + 1) * 128], rhs=fm(hch, ks),
                         start=s, stop=e),
                         reads=[f"q2b{b}", f"hch{ks}"], writes=[pts], tag="q2mm")
                emit("vector", lambda oc=oc, pt=pt: nc.vector.tensor_mul(
                     out=fm(fo, oc), in0=pt[:, :], in1=isbc[:, :]),
                     reads=[pts, "isbc"], writes=[f"xq{oc}"], tag="q2ev")
                rel("A", iA)
            rbL = emit_ln(lambda dc: (fm(fo, dc), f"xq{dc}"), 8, WC,
                          lambda *a: None, bf_in=True)
            RRL, REL = rbL + 2, rbL + 3
            for dc in range(8):
                d = dc % 2
                dma("sync", x3r[d][:, :],
                    xsp[dc * 128:(dc + 1) * 128, c * WC:(c + 1) * WC],
                    [f"x3r{d}"], [], f"x_3{d}", "x3ld2")
                iR, iM = acq("A"), acq("A")
                pR, pRs = pa(iR)
                pM, pMs = pa(iM)
                emit("tensor", lambda pR=pR, RRL=RRL: nc.tensor.matmul(
                     out=pR[:, :], lhsT=ones_r[0:1, :],
                     rhs=rows[0:1, RRL * 1024:RRL * 1024 + 512],
                     start=True, stop=True),
                     reads=[f"row{RRL}", "ones_r"], writes=[pRs])
                emit("tensor", lambda pM=pM, REL=REL: nc.tensor.matmul(
                     out=pM[:, :], lhsT=ones_r[0:1, :],
                     rhs=rows[0:1, REL * 1024:REL * 1024 + 512],
                     start=True, stop=True),
                     reads=[f"row{REL}", "ones_r"], writes=[pMs])
                emit("vector", lambda dc=dc, d=d, pR=pR: nc.vector.tensor_mul(
                     out=t1f[d][:, :], in0=fm(fo, dc), in1=pR[:, :]),
                     reads=[f"xq{dc}", pRs], writes=[f"t1f{d}"])
                emit("vector", lambda d=d, pM=pM: nc.vector.tensor_sub(
                     out=t1f[d][:, :], in0=t1f[d][:, :], in1=pM[:, :]),
                     reads=[f"t1f{d}", pMs], writes=[f"t1f{d}"])
                emit("vector", lambda d=d: nc.vector.tensor_add(
                     out=x3r[d][:, :], in0=x3r[d][:, :], in1=t1f[d][:, :]),
                     reads=[f"t1f{d}", f"x3r{d}"], writes=[f"x3r{d}"])
                rel("A", iR)
                rel("A", iM)
                dma("gpsimd", xsp[dc * 128:(dc + 1) * 128, c * WC:(c + 1) * WC],
                    x3r[d][:, :], [f"spL{c}"], [f"x3r{d}"], f"s_l{d}", "spillL")

        facc2 = [sbl(f"facc2{i}", 128, WC, F32) for i in range(2)]
        x3r = [sbl(f"x3r{i}", 128, WC, F32) for i in range(2)]

        def l1_head(c, cur):
            def get_x3(dc, c=c):
                d = dc % 2
                dma("sync", x3r[d][:, :],
                    xsp[dc * 128:(dc + 1) * 128, c * WC:(c + 1) * WC],
                    [f"x3r{d}"], [], f"x_3{d}", "x3ld")
                return (x3r[d][:, :], f"x3r{d}")
            emit_ln(get_x3, 8, WC, lambda *a: None)
            iR, iM = acq("A"), acq("A")
            pR, pRs = pa(iR)
            pM, pMs = pa(iM)
            emit("tensor", lambda pR=pR: nc.tensor.matmul(
                 out=pR[:, :], lhsT=ones_r[0:1, :],
                 rhs=rows[0:1, 2 * 1024:2 * 1024 + 512], start=True, stop=True),
                 reads=["row2", "ones_r"], writes=[pRs])
            emit("tensor", lambda pM=pM: nc.tensor.matmul(
                 out=pM[:, :], lhsT=ones_r[0:1, :],
                 rhs=rows[0:1, 3 * 1024:3 * 1024 + 512], start=True, stop=True),
                 reads=["row3", "ones_r"], writes=[pMs])
            for dc in range(8):
                d = dc % 2
                dma("sync", x3r[d][:, :],
                    xsp[dc * 128:(dc + 1) * 128, c * WC:(c + 1) * WC],
                    [f"x3r{d}"], [], f"x_3{d}", "x3ld1")
                emit("vector", lambda dc=dc, pR=pR, d=d: nc.vector.tensor_mul(
                     out=t1f[d][:, :], in0=x3r[d][:, :], in1=pR[:, :]),
                     reads=[f"x3r{d}", pRs], writes=[f"t1f{d}"])
                emit("vector", lambda dc=dc, pM=pM, d=d: nc.vector.tensor_sub(
                     out=fm(xnq, dc), in0=t1f[d][:, :], in1=pM[:, :]),
                     reads=[f"t1f{d}", pMs], writes=[f"xnq{dc}"])
            rel("A", iR)
            rel("A", iM)
            quant_act(lambda dc: fm(xnq, dc), lambda dc: f"xnq{dc}", 8, 0,
                      lambda dc: fm(xq, dc), lambda dc: f"xq{dc}", 0)
            for ocb in range(16):
                b = ocb % 2
                dma("sync", d3(qwb[b], 256), colslab(q1T, ocb * 256, (ocb + 1) * 256),
                    [f"qwb{b}"], ["S.q1T"], f"w_q{b}", "q1ld")
                for oci in range(2):
                    oc = ocb * 2 + oci
                    iA = acq("A")
                    pt, pts = pa(iA)
                    for ks in range(8):
                        emit("tensor", lambda b=b, oci=oci, ks=ks, pt=pt, s=(ks == 0),
                             e=(ks == 7): nc.tensor.matmul(out=pt[:, :],
                             lhsT=qwb[b][:, ks * 256 + oci * 128:ks * 256 + (oci + 1) * 128],
                             rhs=fm(xq, ks), start=s, stop=e),
                             reads=[f"qwb{b}", f"xq{ks}"], writes=[pts], tag="q1mm")
                    emit("vector", lambda oc=oc, pt=pt, cur=cur: nc.vector.tensor_mul(
                         out=fm(h1d[cur], oc), in0=pt[:, :], in1=isbc[:, :]),
                         reads=[pts, "isbc"], writes=[f"hd{cur}_{oc}"], tag="q1ev")
                    rel("A", iA)

        for c in range(NCH):
            l1_head(c, c % 2)
            emit("vector", lambda c=c, cur=c % 2: nc.vector.tensor_copy(
                 out=hsv[:, c * 32:(c + 1) * 32].rearrange("p (o w) -> p o w", w=1),
                 in_=h1d[cur][:, :].rearrange("p (o w) -> p o w", w=WC)[:, :, WC - 1:WC]),
                 reads=[f"hd{c % 2}_{oc}" for oc in range(32)],
                 writes=[f"hsv{c}"], tag="hsave")
            if c > 0:
                l1_tail(c - 1, c % 2, (c - 1) % 2)
        l1_tail(NCH - 1, (NCH - 1) % 2, (NCH - 1) % 2)
        pr.barrier()
        pl.close()

    FINAL_SRC = xsp

    po = ExitStack()
    oxc = sb("oxc", 128, 8 * WC, F32, po.enter_context)
    otk = [sb(f"otk{i}", 128, D, F32, po.enter_context) for i in range(2)]
    for c in range(NCH):
        dma("sync", oxc[:, :].rearrange("p (g t) -> p g t", g=8),
            FINAL_SRC.rearrange("(g p) t -> p g t", p=128)[:, :, c * WC:(c + 1) * WC],
            ["oxc"], ["odone"], "x_o", "old")
        for j in range(4):
            b = j % 2
            for dc in range(8):
                i = acq("A")
                pt, pts = pa(i)
                emit("tensor", lambda dc=dc, j=j, pt=pt: nc.tensor.transpose(
                     out=pt[:, 0:128], in_=oxc[:, dc * WC + j * 128:dc * WC + (j + 1) * 128],
                     identity=idf[:, :]), reads=["oxc", "idf"], writes=[pts],
                     tag="oT")
                emit("scalar", lambda dc=dc, b=b, pt=pt: nc.scalar.activation(
                     out=otk[b][:, dc * 128:(dc + 1) * 128], in_=pt[:, 0:128],
                     func=AF.Copy), reads=[pts], writes=[f"otk{b}"], tag="oTe")
                rel("A", i)
            dma("gpsimd", out_t[(c * 4 + j) * 128:(c * 4 + j + 1) * 128, :],
                otk[b][:, :], ["odone"], [f"otk{b}"], f"o_{b}", "ost")
    fin = emit("gpsimd", lambda: nc.gpsimd.nop(), deps=[pr.stream["gpsimd"][-1]],
               tag="final")
    fin.xwaits = tuple((s, n) for s, n in pr.dma_cnt.items()
                       if s.startswith("o_") or s.startswith("s_") or s.startswith("dO"))
    po.close()

    pr.finalize(ctx)
    with nc.Block() as block:
        pr.replay_into(block)
    ctx.close()
    return nc


def kernel(**inputs):
    nc = bass.Bass("TRN2", target_bir_lowering=False, debug=False,
                   detect_race_conditions=False)
    build(nc)
    x = np.ascontiguousarray(np.asarray(inputs["L0_attn_x"], np.float32))
    wmap = {k: np.ascontiguousarray(np.asarray(v, np.float32))
            for k, v in inputs.items() if k != "L0_attn_x"}
    in_maps = []
    for b in range(8):
        m = {"x_in": x[b]}
        m.update(wmap)
        in_maps.append(m)
    res = run_bass_kernel_spmd(nc, in_maps, core_ids=list(range(8)))
    return np.stack([np.asarray(r["out"], np.float32) for r in res.results])
